# version 06
# baseline (speedup 1.0000x reference)
import os
import numpy as np
import ml_dtypes
import concourse.bass as bass
import concourse.mybir as mybir
from concourse.bass_utils import run_bass_kernel_spmd

F32 = mybir.dt.float32
BF16 = mybir.dt.bfloat16
FP8 = mybir.dt.float8e4
AF = mybir.ActivationFunctionType
ALU = mybir.AluOpType
AX = mybir.AxisListType

NT = 2048
T = 4096
D = 2048
HH = 8
DH = 128
CK = 16
WIN = 512
NWIN = T // WIN
CPW = WIN // CK
II = 5632
SCALE = float(DH) ** -0.5
LN_EPS = 1e-5
QS = 224.0

ENGINES = ("sync", "gpsimd", "vector", "scalar", "tensor")
DEBUG_DUMPS = os.environ.get("K_DEBUG_DUMPS") == "1"
K_WINS = int(os.environ.get("K_WINS", "99"))
K_WIN_MODE = os.environ.get("K_WIN_MODE", "full")
K_SCAN_STEPS = int(os.environ.get("K_SCAN_STEPS", "9"))


class PR:
    def __init__(self, nc):
        self.nc = nc
        self.ops = {e: [] for e in ENGINES}
        self.cnt = {}
        self.sems = {}
        self._sem_cms = []

    def sem(self, name):
        if name not in self.sems:
            cm = self.nc.semaphore(name)
            self.sems[name] = cm.__enter__()
            self._sem_cms.append(cm)
            self.cnt[name] = 0
        return self.sems[name]

    def use(self, name, real):
        self.sem(real)
        if not hasattr(self, "amap"):
            self.amap = {}
        self.amap[name] = real

    def _r(self, name):
        return getattr(self, "amap", {}).get(name, name)

    def do(self, eng, fn, wait=(), inc=()):
        wait = tuple((self._r(s), v) for s, v in wait)
        inc = tuple((self._r(s), n) for s, n in inc)
        for s, v in wait:
            assert v <= self.cnt.get(s, 0), f"wait on future {s}={v} (have {self.cnt.get(s, 0)})"
        self.ops[eng].append((tuple(w for w in wait if w[1] > 0), fn, tuple(inc)))
        for s, n in inc:
            self.cnt[s] = self.cnt.get(s, 0) + n

    def now(self, name):
        name = self._r(name)
        return (name, self.cnt.get(name, 0))

    def replay(self, eng, engobj):
        auto_drain = eng in ("vector", "scalar")
        for waits, fn, incs in self.ops[eng]:
            for s, v in waits:
                engobj.wait_ge(self.sems[s], v)
            if auto_drain:
                engobj.drain()
            r = fn(engobj)
            for s, n in incs:
                if r is not None:
                    r.then_inc(self.sems[s], n)
                    r = None
                else:
                    engobj.nop().then_inc(self.sems[s], n)

    def bar(self, tag):
        dma_all = [k for k in self.cnt if k.startswith(("dls", "dlg", "dst"))]
        dma_waits = tuple((s, self.cnt[s]) for s in dma_all if self.cnt[s] > 0)
        for e in ENGINES:
            self.do(e, lambda x: None, wait=dma_waits, inc=((f"bar_{e}", 1),))
        waits = tuple(self.now(f"bar_{e}") for e in ENGINES)
        for e in ENGINES:
            self.do(e, lambda x: None, wait=waits)


def mk_consts():
    c = {}
    c["id_f32"] = np.eye(128, dtype=np.float32)
    c["id_bf"] = np.eye(128).astype(ml_dtypes.bfloat16)
    c["ones_1x128_f32"] = np.ones((1, 128), dtype=np.float32)
    n_idx = np.arange(128) // CK
    t_idx = np.arange(128) % CK
    same = (n_idx[:, None] == n_idx[None, :])
    c["cm_incl"] = (same & (t_idx[:, None] <= t_idx[None, :])).astype(ml_dtypes.bfloat16)
    c["cm_excl"] = (same & (t_idx[:, None] < t_idx[None, :])).astype(ml_dtypes.bfloat16)
    bd = np.zeros((128, 1024), dtype=ml_dtypes.bfloat16)
    for n in range(HH):
        bd[n * CK:(n + 1) * CK, n * 128:(n + 1) * 128] = 1
    c["bd_mask"] = bd
    for n in range(HH):
        m = np.zeros((128, 128), dtype=ml_dtypes.bfloat16)
        m[n * CK + CK - 1, :] = 1
        c[f"sel_ac{n}"] = m
    hs = np.zeros((8, 128), dtype=ml_dtypes.bfloat16)
    for n in range(HH):
        hs[n, n * CK:(n + 1) * CK] = 1
    c["headsel"] = hs
    tm = np.zeros((128, CK), dtype=ml_dtypes.bfloat16)
    for p in range(128):
        tm[p, p % CK] = 1
    c["tmask"] = tm
    for b in range(16):
        m = np.zeros((16, 128), dtype=np.float32)
        m[b, :] = 1
        c[f"selr{b}"] = m
    km = np.zeros((128, 1024), dtype=ml_dtypes.bfloat16)
    for n in range(HH):
        km[:, n * 128 + n * CK:n * 128 + (n + 1) * CK] = 1
    c["kmask8"] = km
    sm = np.zeros((128, 1024), dtype=ml_dtypes.bfloat16)
    for n in range(HH):
        sm[n * CK + CK - 1, n * 128:(n + 1) * 128] = 1
    c["selmask"] = sm
    c["ones128"] = np.ones((128, 128), dtype=ml_dtypes.bfloat16)
    return c


def build_nc():
    nc = bass.Bass("TRN2", target_bir_lowering=False, debug=False)
    consts_np = mk_consts()
    Cdr = {k: nc.inline_tensor(np.ascontiguousarray(v), name=f"c_{k}").ap() for k, v in consts_np.items()}

    dt = nc.dram_tensor
    x0 = dt("x0", [NT, D], F32, kind="ExternalInput").ap()
    ffn_wg = dt("ffn_wg", [II, D], F32, kind="ExternalInput").ap()
    ffn_wu = dt("ffn_wu", [II, D], F32, kind="ExternalInput").ap()
    ffn_wd = dt("ffn_wd", [D, II], F32, kind="ExternalInput").ap()
    aw = {k: dt(f"aw_{k}", [HH * DH, D], F32, kind="ExternalInput").ap() for k in "qkvag"}
    aw_b = dt("aw_b", [HH, D], F32, kind="ExternalInput").ap()
    aw_o = dt("aw_o", [D, HH * DH], F32, kind="ExternalInput").ap()
    a_ba = dt("a_ba", [HH * DH], F32, kind="ExternalInput").ap()
    a_bb = dt("a_bb", [HH], F32, kind="ExternalInput").ap()
    cwt = {k: dt(f"cw_{k}", [HH * DH, 4], F32, kind="ExternalInput").ap() for k in "qkv"}
    cbt = {k: dt(f"cb_{k}", [HH * DH], F32, kind="ExternalInput").ap() for k in "qkv"}
    ln_w = dt("ln_w", [DH], F32, kind="ExternalInput").ap()
    ln_b = dt("ln_b", [DH], F32, kind="ExternalInput").ap()
    y_out = dt("y", [NT, D], F32, kind="ExternalOutput").ap()
    if DEBUG_DUMPS:
        dbg = {nm: dt(f"dbg_{nm}", shp, BF16, kind="ExternalOutput").ap()
               for nm, shp in (("x1", [NT, D]), ("ag", [T, D]), ("rs", [NT, D]), ("x2", [NT, D]),
                               ("rsin", [T, D]), ("kw", [128, 4096]), ("qw", [128, 4096]),
                               ("vw", [128, 4096]), ("nla", [128, 4096]), ("t1e", [128, 4096]),
                               ("gw", [128, 4096]), ("og", [128, 4096]), ("Sq", [128, 1024]),
                               ("xlnw", [128, 8192]),
                               ("ahat", [128, 128]), ("rash", [128, 128]), ("cs", [128, 128]),
                               ("gms", [128, 128]), ("gqks", [128, 128]), ("ms", [128, 128]),
                               ("p1s", [128, 128]), ("vt", [128, 128]), ("rt", [128, 128]),
                               ("uu", [128, 128]), ("oraw", [128, 128]), ("ogs", [128, 128]),
                               ("ksscan", [128, 128]), ("acbc", [128, 1024]), ("betacol", [128, 1]),
                               ("lnwbc", [128, 128]), ("lnbbc", [128, 128]), ("olnt", [128, 128]),
                               ("gss", [128, 128]), ("lnrow", [1, 256]), ("onesc", [1, 128]), ("lnwbc2", [128, 128]),
                               ("pa_mm", [128, 128]), ("lnrow2", [1, 256]), ("onesc2", [1, 128]),
                               ("nlas", [128, 128]))}

    x1bf = dt("x1bf", [NT, D], BF16, kind="Internal").ap()
    x2bf = dt("x2bf", [NT, D], BF16, kind="Internal").ap()
    ag_in = dt("ag_in", [NT, D], BF16, kind="Internal").ap()
    ag_out = dt("ag_out", [T, D], BF16, kind="Internal").ap()
    rs_in = dt("rs_in", [T, D], BF16, kind="Internal").ap()
    rs_out = dt("rs_out", [NT, D], BF16, kind="Internal").ap()
    wtmpA = dt("wtmpA", [II, D], BF16, kind="Internal").ap()
    wtmpB = dt("wtmpB", [D, II], BF16, kind="Internal").ap()
    wg_fm8 = dt("wg_fm8", [D, II], FP8, kind="Internal").ap()
    wu_fm8 = dt("wu_fm8", [D, II], FP8, kind="Internal").ap()
    wd_fm8 = dt("wd_fm8", [II, D], FP8, kind="Internal").ap()
    awfm = {k: dt(f"awfm_{k}", [D, HH * DH], BF16, kind="Internal").ap() for k in "qkvag"}
    awfm_o = dt("awfm_o", [HH * DH, D], BF16, kind="Internal").ap()

    P = PR(nc)
    for e in ENGINES:
        P.sem(f"bar_{e}")
    for s in ("dls", "dlg", "dst", "v", "s", "t", "cc"):
        P.sem(s)
    nowv = P.now

    import contextlib
    stack = contextlib.ExitStack()
    sb = lambda name, shape, dtp: stack.enter_context(nc.sbuf_tensor(name, shape, dtp))[:]
    ps = lambda name, shape, dtp: stack.enter_context(nc.psum_tensor(name, shape, dtp))[:]

    arena_bf = sb("arena_bf", [128, 24576], BF16)
    arena_f8 = sb("arena_f8", [128, 55296], FP8)
    xt32 = sb("xt32", [128, D], F32)
    xln32 = sb("xln32", [128, D], F32)
    sq_dump = sb("sq_dump", [128, D], BF16)
    x2tile = sb("x2tile", [128, D], BF16)
    qscl_pk = sb("qscl_pk", [128, 16], F32)
    rms_pk = sb("rms_pk", [128, 16], F32)
    tsq_pk = sb("tsq_pk", [128, 16], F32)
    tsd_pk = sb("tsd_pk", [128, 16], F32)
    row16 = sb("row16", [16, 128], F32)
    rmaxbc = sb("rmaxbc", [128, D], BF16)
    ddbc = sb("ddbc", [128, D], BF16)
    tqbc = sb("tqbc", [128, 512], BF16)
    wstr1 = sb("wstr1", [128, 2048], FP8)
    wstr2 = sb("wstr2", [128, 2048], FP8)
    wstr3 = sb("wstr3", [128, 512], FP8)
    wsc_g = sb("wsc_g", [128, 44], F32)
    wsc_u = sb("wsc_u", [128, 44], F32)
    wsc_d = sb("wsc_d", [128, 16], F32)
    consts = {}
    for k in consts_np:
        dd = BF16 if consts_np[k].dtype == ml_dtypes.bfloat16 else F32
        consts[k] = sb(f"k_{k}", list(consts_np[k].shape), dd)
    xlnw = sb("xlnw", [128, 16 * WIN], BF16)
    wpr1 = sb("wpr1", [128, 2048], BF16)
    wpr2 = sb("wpr2", [128, 2048], BF16)
    wbsb = sb("wbsb", [128, 128], BF16)
    convw = sb("convw", [128, 96], F32)
    convb = sb("convb", [128, 24], F32)
    negba = sb("negba", [128, 8], F32)
    bbcol = sb("bbcol", [8, 1], F32)
    lnwbc = sb("lnwbc", [128, 128], BF16)
    lnbbc = sb("lnbbc", [128, 128], BF16)
    lnrow = sb("lnrow", [1, 2 * DH], F32)
    halos = sb("halos", [128, 72], BF16)
    prbuf = sb("prbuf", [128, WIN + 3], BF16)
    ktmp = sb("ktmp", [128, WIN], BF16)
    betacm = sb("betacm", [8, WIN], BF16)
    Sq = sb("Sq", [128, 1024], BF16)
    wdiag = sb("wdiag", [128, 1024], BF16)
    acbc = sb("acbc", [128, 1024], BF16)
    nlas = sb("nlas", [128, 128], BF16)
    t1es = sb("t1es", [128, 128], BF16)
    cs_ = sb("cs_", [128, 128], BF16)
    vs_ = sb("vs_", [128, 128], BF16)
    gs_ = sb("gs_", [128, 128], BF16)
    ks_scan = sb("ks_scan", [128, 128], BF16)
    gm_s = sb("gm_s", [128, 128], BF16)
    gqk_s = sb("gqk_s", [128, 128], BF16)
    rash = sb("rash", [128, 128], BF16)
    ahat = sb("ahat", [128, 128], BF16)
    mts = sb("mts", [128, 128], BF16)
    p1ts = sb("p1ts", [128, 128], BF16)
    ms_ = sb("ms_", [128, 128], F32)
    p1s_ = sb("p1s_", [128, 128], F32)
    vt_ = sb("vt_", [128, 128], F32)
    rt_ = sb("rt_", [128, 128], F32)
    uu = sb("uu", [128, 128], BF16)
    yy = sb("yy", [128, 128], F32)
    oraw = sb("oraw", [128, 128], F32)
    olnt = sb("olnt", [128, 128], F32)
    o2t = sb("o2t", [128, 128], BF16)
    ogs = sb("ogs", [128, 128], BF16)
    betacol = sb("betacol", [128, 1], F32)
    bsc1 = sb("bsc1", [128, CK], F32)
    stat1 = sb("stat1", [128, 1], F32)
    stat2 = sb("stat2", [128, 1], F32)
    stat3 = sb("stat3", [128, 1], F32)
    stat4 = sb("stat4", [128, 1], F32)
    epscol = sb("epscol", [128, 1], F32)
    optile = sb("optile", [128, D], BF16)

    pA = ps("pA", [128, 512], F32)
    pB = ps("pB", [128, 512], F32)
    pC = ps("pC", [128, 512], F32)
    pD = ps("pD", [128, 512], F32)
    pH = ps("pH", [128, 512], F32)
    pT = ps("pT", [128, 1024], BF16)
    pS = ps("pS", [128, 1024], F32)

    tgrp = arena_bf[:, 0:22528]
    tmaxrun = arena_bf[:, 22528:23040]
    t1ew = arena_bf[:, 0:4096]
    gww = arena_bf[:, 4096:8192]
    ogw = arena_bf[:, 8192:12288]
    ks8 = arena_bf[:, 12288:13312]
    qs8 = arena_bf[:, 13312:14336]
    ah8 = arena_bf[:, 14336:15360]
    wide32 = arena_bf[:, 12288:23552].bitcast(F32)
    wbf_t = arena_bf[:, 0:5632]
    qxfm8 = arena_f8[:, 0:32768]
    qtg8 = arena_f8[:, 32768:55296]
    abf8 = arena_f8.bitcast(BF16)
    kwv = abf8[:, 0:4096]
    qwv = abf8[:, 4096:8192]
    vwv = abf8[:, 8192:12288]
    nlaw = abf8[:, 12288:16384]
    wq8t = arena_f8[:, 0:5632]

    vw4 = lambda t_: t_.rearrange("p (c n t) -> p c n t", c=CPW, n=HH, t=CK)

    def ln_tile(src_ap, ti, out32, src_is_bf):
        ld = "dlg" if src_is_bf else "dls"
        eng = "gpsimd" if src_is_bf else "sync"
        P.do(eng, lambda e, s=src_ap, i=ti: e.dma_start(out=xt32, in_=s[i * 128:(i + 1) * 128, :]),
             wait=(nowv("v"), nowv("s")), inc=((ld, 16),))
        P.do("vector", lambda v: v.tensor_reduce(out=stat1[:, 0:1], in_=xt32, axis=AX.X, op=ALU.add),
             wait=(nowv(ld),))
        P.do("vector", lambda v: v.scalar_tensor_tensor(out=sq_dump, in0=xt32, scalar=1.0, in1=xt32,
                                                        op0=ALU.mult, op1=ALU.mult,
                                                        accum_out=stat2[:, 0:1]))
        P.do("vector", lambda v: v.drain())
        P.do("vector", lambda v: v.tensor_scalar(out=stat1[:, 0:1], in0=stat1[:, 0:1], scalar1=1.0 / D,
                                                 scalar2=None, op0=ALU.mult))
        P.do("vector", lambda v: v.tensor_scalar(out=stat2[:, 0:1], in0=stat2[:, 0:1], scalar1=1.0 / D,
                                                 scalar2=None, op0=ALU.mult))
        P.do("vector", lambda v: v.drain())
        P.do("vector", lambda v: v.tensor_scalar(out=stat3[:, 0:1], in0=stat1[:, 0:1],
                                                 scalar1=stat1[:, 0:1], scalar2=None, op0=ALU.mult))
        P.do("vector", lambda v: v.drain())
        P.do("vector", lambda v: v.tensor_tensor(out=stat2[:, 0:1], in0=stat2[:, 0:1], in1=stat3[:, 0:1],
                                                 op=ALU.subtract), inc=(("v", 1),))
        P.do("scalar", lambda s: s.activation(out=stat4[:, 0:1], in_=stat2[:, 0:1], func=AF.Ln,
                                              bias=epscol[:, 0:1], scale=1.0), wait=(nowv("v"),), inc=(("s", 1),))
        P.do("vector", lambda v: v.tensor_scalar(out=stat4[:, 0:1], in0=stat4[:, 0:1], scalar1=-0.5,
                                                 scalar2=None, op0=ALU.mult), wait=(nowv("s"),),
             inc=(("v", 1),))
        P.do("scalar", lambda s: s.activation(out=stat3[:, 0:1], in_=stat4[:, 0:1], func=AF.Exp),
             wait=(nowv("v"),), inc=(("s", 1),))
        P.do("vector", lambda v: v.drain(), wait=(nowv("s"),))
        P.do("vector", lambda v, o=out32: v.tensor_scalar(out=o, in0=xt32, scalar1=stat1[:, 0:1],
                                                          scalar2=stat3[:, 0:1], op0=ALU.subtract,
                                                          op1=ALU.mult), inc=(("v", 1),))

    def prep_quant(nat_ap, rows, cols, qbf_dram, pk_sb):
        for i in range(rows // 128):
            P.do("sync", lambda e, i=i: e.dma_start(out=wide32[:, 0:cols],
                                                    in_=nat_ap[i * 128:(i + 1) * 128, :]),
                 wait=(nowv("v"), nowv("dst")), inc=(("dls", 16),))
            P.do("vector", lambda v: v.tensor_reduce(out=stat1[:, 0:1], in_=wide32[:, 0:cols], axis=AX.X,
                                                     op=ALU.max, apply_absolute_value=True),
                 wait=(nowv("dls"),))
            P.do("vector", lambda v: v.drain())
            P.do("vector", lambda v: v.reciprocal(out=stat2[:, 0:1], in_=stat1[:, 0:1]))
            P.do("vector", lambda v: v.drain())
            P.do("vector", lambda v: v.tensor_scalar(out=stat3[:, 0:1], in0=stat2[:, 0:1], scalar1=QS,
                                                     scalar2=None, op0=ALU.mult))
            P.do("vector", lambda v: v.tensor_scalar(out=stat4[:, 0:1], in0=stat1[:, 0:1],
                                                     scalar1=1.0 / QS, scalar2=None, op0=ALU.mult))
            P.do("vector", lambda v: v.drain(), inc=(("v", 1),))
            P.do("vector", lambda v, i=i: v.tensor_copy(out=pk_sb[:, i:i + 1], in_=stat4[:, 0:1]))
            P.do("vector", lambda v: v.tensor_scalar(out=wq8t[:, 0:cols], in0=wide32[:, 0:cols],
                                                     scalar1=stat3[:, 0:1], scalar2=None, op0=ALU.mult))
            P.do("vector", lambda v: v.tensor_copy(out=wbf_t[:, 0:cols], in_=wq8t[:, 0:cols]),
                 inc=(("v", 1),))
            P.do("sync", lambda e, i=i: e.dma_start(out=qbf_dram[i * 128:(i + 1) * 128, :],
                                                    in_=wbf_t[:, 0:cols]),
                 wait=(nowv("v"),), inc=(("dst", 16),))
        P.bar("pq")

    def prep_transpose(qbf_dram, rows, cols, out_dram, out_fp8):
        for f in range(cols // 128):
            P.do("sync", lambda e, f=f: e.dma_start(out=wbf_t[:, 0:rows],
                                                    in_=qbf_dram[:, f * 128:(f + 1) * 128], transpose=True),
                 wait=(nowv("v"), nowv("dst")), inc=(("dls", 16),))
            if out_fp8:
                P.do("vector", lambda v: v.tensor_copy(out=wq8t[:, 0:rows], in_=wbf_t[:, 0:rows]),
                     wait=(nowv("dls"),), inc=(("v", 1),))
                P.do("sync", lambda e, f=f: e.dma_start(out=out_dram[f * 128:(f + 1) * 128, :],
                                                        in_=wq8t[:, 0:rows]),
                     wait=(nowv("v"),), inc=(("dst", 16),))
            else:
                P.do("vector", lambda v: v.drain(), wait=(nowv("dls"),), inc=(("v", 1),))
                P.do("sync", lambda e, f=f: e.dma_start(out=out_dram[f * 128:(f + 1) * 128, :],
                                                        in_=wbf_t[:, 0:rows]),
                     wait=(nowv("v"),), inc=(("dst", 16),))
        P.bar("pt")

    def prep_cast_bf(nat_ap, rows, cols, out_dram):
        for i in range((rows + 127) // 128):
            r = min(128, rows - i * 128)
            P.do("gpsimd", lambda g, i=i, r=r: g.dma_start(out=wbf_t[0:r, 0:cols],
                                                           in_=nat_ap[i * 128:i * 128 + r, :]),
                 wait=(nowv("dst"),), inc=(("dlg", 16),))
            P.do("sync", lambda e, i=i, r=r: e.dma_start(out=out_dram[i * 128:i * 128 + r, :],
                                                         in_=wbf_t[0:r, 0:cols]),
                 wait=(nowv("dlg"),), inc=(("dst", 16),))
        P.bar("pc")

    def phase_weights():
        P.do("gpsimd", lambda g: g.memset(epscol, 1e-5))
        P.bar("eps")
        for k in consts_np:
            P.do("sync", lambda e, k=k: e.dma_start(out=consts[k], in_=Cdr[k][:, :]), inc=(("dls", 16),))
        prep_quant(ffn_wg, II, D, wtmpA, wsc_g)
        prep_transpose(wtmpA, II, D, wg_fm8, True)
        prep_quant(ffn_wu, II, D, wtmpA, wsc_u)
        prep_transpose(wtmpA, II, D, wu_fm8, True)
        prep_quant(ffn_wd, D, II, wtmpB, wsc_d)
        prep_transpose(wtmpB, D, II, wd_fm8, True)
        for k in "qkvag":
            prep_cast_bf(aw[k], HH * DH, D, wtmpA[0:HH * DH, :])
            prep_transpose(wtmpA[0:HH * DH, :], HH * DH, D, awfm[k], False)
        prep_cast_bf(aw_o, D, HH * DH, wtmpB[0:D, 0:HH * DH])
        for f in range(HH):
            P.do("sync", lambda e, f=f: e.dma_start(out=wbf_t[:, 0:D],
                                                    in_=wtmpB[0:D, f * 128:(f + 1) * 128], transpose=True),
                 wait=(nowv("v"), nowv("dst")), inc=(("dls", 16),))
            P.do("vector", lambda v: v.drain(), wait=(nowv("dls"),), inc=(("v", 1),))
            P.do("sync", lambda e, f=f: e.dma_start(out=awfm_o[f * 128:(f + 1) * 128, :],
                                                    in_=wbf_t[:, 0:D]),
                 wait=(nowv("v"),), inc=(("dst", 16),))
        P.bar("pw")

    def phase_ffn(call):
        src = x0 if call == 0 else x2bf
        src_bf = (call == 1)
        qx4 = qxfm8.rearrange("p (f k) -> p f k", f=16, k=NT)
        for i in range(16):
            ln_tile(src, i, xln32, src_bf)
            P.do("vector", lambda v: v.tensor_reduce(out=stat1[:, 0:1], in_=xln32, axis=AX.X,
                                                     op=ALU.max, apply_absolute_value=True))
            P.do("vector", lambda v: v.drain())
            P.do("vector", lambda v: v.reciprocal(out=stat2[:, 0:1], in_=stat1[:, 0:1]))
            P.do("vector", lambda v: v.drain())
            P.do("vector", lambda v, i=i: v.tensor_scalar(out=qscl_pk[:, i:i + 1], in0=stat2[:, 0:1],
                                                          scalar1=QS, scalar2=None, op0=ALU.mult))
            P.do("vector", lambda v, i=i: v.tensor_scalar(out=rms_pk[:, i:i + 1], in0=stat1[:, 0:1],
                                                          scalar1=1.0 / QS, scalar2=None, op0=ALU.mult))
            P.do("vector", lambda v: v.drain())
            P.do("vector", lambda v, i=i: v.tensor_scalar(out=xln32, in0=xln32,
                                                          scalar1=qscl_pk[:, i:i + 1], scalar2=None,
                                                          op0=ALU.mult), inc=(("v", 1),))
            for f in range(16):
                P.do("tensor", lambda t, f=f: t.transpose(pH[:, 0:128], xln32[:, f * 128:(f + 1) * 128],
                                                          consts["id_f32"]),
                     wait=(nowv("v"), nowv("s")), inc=(("t", 1),))
                P.do("scalar", lambda s, f=f, i=i: s.copy(out=qx4[:, f, i * 128:(i + 1) * 128],
                                                          in_=pH[:, 0:128]),
                     wait=(nowv("t"),), inc=(("s", 1),))
        P.do("tensor", lambda t: t.transpose(pH[0:16, 0:128], rms_pk[:, 0:16], consts["id_f32"]),
             wait=(nowv("s"), nowv("v")), inc=(("t", 1),))
        P.do("vector", lambda v: v.tensor_copy(out=row16[0:16, 0:128], in_=pH[0:16, 0:128]),
             wait=(nowv("t"),), inc=(("v", 1),))
        for b in range(16):
            P.do("tensor", lambda t, b=b: t.matmul(pA[:, 0:128], consts[f"selr{b}"],
                                                   row16[0:16, 0:128], start=True, stop=True),
                 wait=(nowv("v"),), inc=(("t", 1),))
            P.do("vector", lambda v, b=b: v.tensor_copy(out=rmaxbc[:, b * 128:(b + 1) * 128],
                                                        in_=pA[:, 0:128]),
                 wait=(nowv("t"),), inc=(("v", 1),))
        P.do("tensor", lambda t: t.transpose(pH[0:16, 0:128], wsc_d[:, 0:16], consts["id_f32"]),
             wait=(nowv("v"),), inc=(("t", 1),))
        P.do("vector", lambda v: v.tensor_copy(out=row16[0:16, 0:128], in_=pH[0:16, 0:128]),
             wait=(nowv("t"),), inc=(("v", 1),))
        for b in range(16):
            P.do("tensor", lambda t, b=b: t.matmul(pA[:, 0:128], consts[f"selr{b}"],
                                                   row16[0:16, 0:128], start=True, stop=True),
                 wait=(nowv("v"),), inc=(("t", 1),))
            P.do("vector", lambda v, b=b: v.tensor_copy(out=ddbc[:, b * 128:(b + 1) * 128], in_=pA[:, 0:128]),
                 wait=(nowv("t"),), inc=(("v", 1),))
        P.bar(f"fA{call}")

        wgv = wg_fm8.rearrange("(fc p) i -> p fc i", p=128)
        wuv = wu_fm8.rearrange("(fc p) i -> p fc i", p=128)
        w1v = wstr1.rearrange("p (fc i) -> p fc i", fc=16, i=128)
        w2v = wstr2.rearrange("p (fc i) -> p fc i", fc=16, i=128)
        for tg in range(4):
            toks = slice(tg * 512, (tg + 1) * 512)
            for j in range(44):
                P.do("sync", lambda e, j=j: e.dma_start(out=w1v, in_=wgv[:, :, j * 128:(j + 1) * 128]),
                     wait=(nowv("t"),), inc=(("dls", 16),))
                P.do("sync", lambda e, j=j: e.dma_start(out=w2v, in_=wuv[:, :, j * 128:(j + 1) * 128]),
                     inc=(("dls", 16),))
                for f in range(16):
                    P.do("tensor",
                         lambda t, f=f, tk=toks: t.matmul(pA, wstr1[:, f * 128:(f + 1) * 128],
                                                          qx4[:, f, tk], start=(f == 0), stop=(f == 15)),
                         wait=(nowv("dls"), nowv("v")) if f == 0 else (),
                         inc=(("t", 1),) if f == 15 else ())
                for f in range(16):
                    P.do("tensor",
                         lambda t, f=f, tk=toks: t.matmul(pB, wstr2[:, f * 128:(f + 1) * 128],
                                                          qx4[:, f, tk], start=(f == 0), stop=(f == 15)),
                         inc=(("t", 1),) if f == 15 else ())
                P.do("vector", lambda v, j=j, tk=toks: v.scalar_tensor_tensor(
                    out=xln32[:, 0:512], in0=pA, scalar=wsc_g[:, j:j + 1], in1=rmaxbc[:, tk],
                    op0=ALU.mult, op1=ALU.mult), wait=(nowv("t"),), inc=(("v", 1),))
                P.do("scalar", lambda s: s.activation(out=xt32[:, 0:512], in_=xln32[:, 0:512],
                                                      func=AF.Silu), wait=(nowv("v"),), inc=(("s", 1),))
                P.do("vector", lambda v, j=j, tk=toks: v.scalar_tensor_tensor(
                    out=xln32[:, 512:1024], in0=pB, scalar=wsc_u[:, j:j + 1], in1=rmaxbc[:, tk],
                    op0=ALU.mult, op1=ALU.mult), inc=(("v", 1),))
                P.do("vector", lambda v, j=j: v.tensor_tensor(out=tgrp[:, j * 512:(j + 1) * 512],
                                                              in0=xt32[:, 0:512], in1=xln32[:, 512:1024],
                                                              op=ALU.mult), wait=(nowv("s"),),
                     inc=(("v", 1),))
                P.do("scalar", lambda s, j=j: s.activation(out=xt32[:, 512:1024],
                                                           in_=tgrp[:, j * 512:(j + 1) * 512],
                                                           func=AF.Abs), wait=(nowv("v"),),
                     inc=(("s", 1),))
                if j == 0:
                    P.do("vector", lambda v: v.tensor_copy(out=tmaxrun, in_=xt32[:, 512:1024]),
                         wait=(nowv("s"),), inc=(("v", 1),))
                else:
                    P.do("vector", lambda v: v.tensor_tensor(out=tmaxrun, in0=tmaxrun,
                                                             in1=xt32[:, 512:1024], op=ALU.max),
                         wait=(nowv("s"),), inc=(("v", 1),))
            P.do("vector", lambda v: v.drain(), inc=(("v", 1),))
            for r in range(4):
                P.do("tensor", lambda t, r=r: t.transpose(pT[:, 0:128], tmaxrun[:, r * 128:(r + 1) * 128],
                                                          consts["id_bf"]),
                     wait=(nowv("v"),), inc=(("t", 1),))
                P.do("vector", lambda v, r=r: v.tensor_reduce(out=stat1[:, 0:1], in_=pT[:, 0:128],
                                                              axis=AX.X, op=ALU.max), wait=(nowv("t"),))
                P.do("vector", lambda v: v.drain())
                P.do("vector", lambda v: v.reciprocal(out=stat2[:, 0:1], in_=stat1[:, 0:1]))
                P.do("vector", lambda v: v.drain())
                P.do("vector", lambda v, r=r, tg=tg: v.tensor_scalar(
                    out=tsq_pk[:, tg * 4 + r:tg * 4 + r + 1], in0=stat2[:, 0:1], scalar1=QS,
                    scalar2=None, op0=ALU.mult))
                P.do("vector", lambda v, r=r, tg=tg: v.tensor_scalar(
                    out=tsd_pk[:, tg * 4 + r:tg * 4 + r + 1], in0=stat1[:, 0:1], scalar1=1.0 / QS,
                    scalar2=None, op0=ALU.mult), inc=(("v", 1),))
            P.do("tensor", lambda t, tg=tg: t.transpose(pH[0:4, 0:128], tsq_pk[:, tg * 4:tg * 4 + 4],
                                                        consts["id_f32"]),
                 wait=(nowv("v"),), inc=(("t", 1),))
            P.do("vector", lambda v: v.tensor_copy(out=row16[0:4, 0:128], in_=pH[0:4, 0:128]),
                 wait=(nowv("t"),), inc=(("v", 1),))
            for r in range(4):
                P.do("tensor", lambda t, r=r: t.matmul(pA[:, 0:128], consts[f"selr{r}"],
                                                       row16[0:16, 0:128], start=True, stop=True),
                     wait=(nowv("v"),), inc=(("t", 1),))
                P.do("vector", lambda v, r=r: v.tensor_copy(out=tqbc[:, r * 128:(r + 1) * 128],
                                                            in_=pA[:, 0:128]), wait=(nowv("t"),),
                     inc=(("v", 1),))
            for j in range(44):
                P.do("vector", lambda v, j=j: v.tensor_tensor(out=qtg8[:, j * 512:(j + 1) * 512],
                                                              in0=tgrp[:, j * 512:(j + 1) * 512],
                                                              in1=tqbc, op=ALU.mult),
                     inc=(("v", 1),) if j == 43 else ())
            for e4 in range(4):
                ecols = slice(e4 * 512, (e4 + 1) * 512)
                for j in range(44):
                    P.do("sync", lambda e, j=j, ec=ecols: e.dma_start(
                        out=wstr3, in_=wd_fm8[j * 128:(j + 1) * 128, ec]),
                        wait=(nowv("t"),), inc=(("dls", 16),))
                    for r in range(4):
                        pb = (pA, pB, pC, pD)[r]
                        P.do("tensor", lambda t, j=j, r=r, pb=pb: t.matmul(
                            pb, qtg8[:, j * 512 + r * 128:j * 512 + (r + 1) * 128], wstr3,
                            start=(j == 0), stop=(j == 43)),
                             wait=(nowv("dls"), nowv("v")) if (j == 0 and r == 0) else
                             ((nowv("dls"),) if r == 0 else ()),
                             inc=(("t", 1),) if r == 3 else ())
                for r in range(4):
                    pb = (pA, pB, pC, pD)[r]
                    ti = tg * 4 + r
                    P.do("vector", lambda v, pb=pb, ti=ti, ec=ecols: v.scalar_tensor_tensor(
                        out=xln32[:, 0:512], in0=pb, scalar=tsd_pk[:, ti:ti + 1],
                        in1=ddbc[:, ec], op0=ALU.mult, op1=ALU.mult),
                         wait=(nowv("t"),))
                    if call == 0:
                        P.do("sync", lambda e, ti=ti, ec=ecols: e.dma_start(
                            out=xt32[:, 0:512], in_=x0[ti * 128:(ti + 1) * 128, ec]),
                            wait=(nowv("v"),), inc=(("dls", 16),))
                        P.do("vector", lambda v: v.drain(), wait=(nowv("dls"),))
                        P.do("vector", lambda v: v.scalar_tensor_tensor(
                            out=sq_dump[:, 0:512], in0=xln32[:, 0:512], scalar=0.5, in1=xt32[:, 0:512],
                            op0=ALU.mult, op1=ALU.add), inc=(("v", 1),))
                        P.do("sync", lambda e, ti=ti, ec=ecols: e.dma_start(
                            out=x1bf[ti * 128:(ti + 1) * 128, ec], in_=sq_dump[:, 0:512]),
                            wait=(nowv("v"),), inc=(("dst", 16),))
                    else:
                        P.do("gpsimd", lambda g, ti=ti, ec=ecols: g.dma_start(
                            out=xt32[:, 0:512], in_=x2bf[ti * 128:(ti + 1) * 128, ec]),
                            wait=(nowv("v"),), inc=(("dlg", 16),))
                        P.do("vector", lambda v: v.drain(), wait=(nowv("dlg"),))
                        P.do("vector", lambda v: v.scalar_tensor_tensor(
                            out=xln32[:, 512:1024], in0=xln32[:, 0:512], scalar=0.5, in1=xt32[:, 0:512],
                            op0=ALU.mult, op1=ALU.add), inc=(("v", 1),))
                        P.do("sync", lambda e, ti=ti, ec=ecols: e.dma_start(
                            out=y_out[ti * 128:(ti + 1) * 128, ec], in_=xln32[:, 512:1024]),
                            wait=(nowv("v"),), inc=(("dst", 16),))
        P.bar(f"fB{call}")

    def phase_exchange1():
        for i in range(16):
            ln_tile(x1bf, i, xln32, True)
            P.do("vector", lambda v: v.tensor_copy(out=sq_dump, in_=xln32), inc=(("v", 1),))
            P.do("sync", lambda e, i=i: e.dma_start(out=ag_in[i * 128:(i + 1) * 128, :], in_=sq_dump),
                 wait=(nowv("v"),), inc=(("dst", 16),))
        P.bar("e1a")
        P.do("gpsimd", lambda g: g.collective_compute(
            "AllGather", ALU.bypass, replica_groups=[[0, 1], [2, 3], [4, 5], [6, 7]],
            ins=[ag_in[:, :]], outs=[ag_out[:, :]]), inc=(("cc", 1),))
        P.do("gpsimd", lambda g: None, wait=(nowv("cc"),))
        P.bar("e1b")

    def attn_setup():
        P.do("gpsimd", lambda g: g.dma_start(out=sq_dump[0:8, :], in_=aw_b[:, :]), inc=(("dlg", 16),))
        for f in range(16):
            P.do("tensor", lambda t, f=f: t.transpose(pT[:, 0:8], sq_dump[0:8, f * 128:(f + 1) * 128],
                                                      consts["id_bf"][0:8, 0:8]),
                 wait=(nowv("dlg"), nowv("s")), inc=(("t", 1),))
            P.do("scalar", lambda s, f=f: s.copy(out=wbsb[:, f * 8:(f + 1) * 8], in_=pT[:, 0:8]),
                 wait=(nowv("t"),), inc=(("s", 1),))
        for mi, k in enumerate("qkv"):
            P.do("sync", lambda e, mi=mi, k=k: e.dma_start(
                out=convw.rearrange("p (m n j) -> p m n j", m=3, n=8, j=4)[:, mi, :, :],
                in_=cwt[k].rearrange("(n p) j -> p n j", p=128)), inc=(("dls", 16),))
            P.do("sync", lambda e, mi=mi, k=k: e.dma_start(
                out=convb.rearrange("p (m n) -> p m n", m=3, n=8)[:, mi, :],
                in_=cbt[k].rearrange("(n p) -> p n", p=128), allow_slow_non_contiguous=True),
                inc=(("dls", 16),))
        P.do("sync", lambda e: e.dma_start(out=negba, in_=a_ba.rearrange("(n p) -> p n", p=128),
                                           allow_slow_non_contiguous=True), inc=(("dls", 16),))
        P.do("vector", lambda v: v.tensor_scalar(out=negba, in0=negba, scalar1=-1.0, scalar2=None,
                                                 op0=ALU.mult), wait=(nowv("dls"),), inc=(("v", 1),))
        P.do("sync", lambda e: e.dma_start(out=bbcol[0:8, 0:1],
                                           in_=a_bb.rearrange("(n one) -> n one", one=1)),
             inc=(("dls", 16),))
        P.do("sync", lambda e: e.dma_start(out=lnrow[0:1, 0:DH], in_=ln_w.rearrange("(o d) -> o d", o=1)),
             inc=(("dls", 16),))
        P.do("sync", lambda e: e.dma_start(out=lnrow[0:1, DH:2 * DH],
                                           in_=ln_b.rearrange("(o d) -> o d", o=1)), inc=(("dls", 16),))
        if DEBUG_DUMPS:
            P.do("gpsimd", lambda g: g.dma_start(out=dbg["lnrow2"][:, :], in_=lnrow),
                 wait=(nowv("dls"),), inc=(("dlg", 16),))
            P.do("gpsimd", lambda g: g.dma_start(out=dbg["onesc2"][:, :], in_=consts["ones_1x128_f32"]),
                 inc=(("dlg", 16),))
        P.do("tensor", lambda t: t.matmul(pA[:, 0:128], consts["ones_1x128_f32"][0:1, :],
                                          lnrow[0:1, 0:DH], start=True, stop=True),
             wait=(nowv("dls"),), inc=(("t", 1),))
        P.do("vector", lambda v: v.tensor_copy(out=lnwbc, in_=pA[:, 0:128]), wait=(nowv("t"),),
             inc=(("v", 1),))
        if DEBUG_DUMPS:
            P.do("scalar", lambda sc: sc.copy(out=o2t, in_=pA[:, 0:128]), wait=(nowv("t"),),
                 inc=(("s", 1),))
            P.do("sync", lambda e: e.dma_start(out=dbg["pa_mm"][:, :], in_=o2t),
                 wait=(nowv("s"),), inc=(("dst", 16),))
        P.do("tensor", lambda t: t.matmul(pA[:, 0:128], consts["ones_1x128_f32"][0:1, :],
                                          lnrow[0:1, DH:2 * DH], start=True, stop=True),
             wait=(nowv("v"),), inc=(("t", 1),))
        P.do("vector", lambda v: v.tensor_copy(out=lnbbc, in_=pA[:, 0:128]), wait=(nowv("t"),),
             inc=(("v", 1),))
        if DEBUG_DUMPS:
            P.do("sync", lambda e: e.dma_start(out=dbg["lnwbc2"][:, :], in_=lnwbc),
                 wait=(nowv("v"),), inc=(("dst", 16),))
        P.do("gpsimd", lambda g: g.memset(Sq, 0.0))
        P.do("gpsimd", lambda g: g.memset(halos, 0.0))
        P.do("gpsimd", lambda g: g.memset(wdiag, 0.0))
        P.bar("as")

    def attn_window(w):
        kw4, qw4, vw4_, nla4, t1e4, gw4 = (vw4(t_) for t_ in (kwv, qwv, vwv, nlaw, t1ew, gww))
        for f in range(16):
            if K_WIN_MODE == "notrans":
                P.do("sync", lambda e, f=f, w=w: e.dma_start(
                    out=xlnw[:, f * WIN:(f + 1) * WIN],
                    in_=ag_out[w * 128 + f:w * 128 + f + 128, 0:WIN]),
                    inc=(("dls", 16),))
            else:
                P.do("sync", lambda e, f=f, w=w: e.dma_start(
                    out=xlnw[:, f * WIN:(f + 1) * WIN],
                    in_=ag_out[w * WIN:(w + 1) * WIN, f * 128:(f + 1) * 128], transpose=True),
                    inc=(("dls", 16),))
        mats = [("q", awfm["q"]), ("k", awfm["k"]), ("v", awfm["v"]), ("g", awfm["g"]), ("a", awfm["a"])]
        for mi, (mk, wdram) in enumerate(mats):
            wdv = wdram.rearrange("(fc p) i -> p fc i", p=128)
            for n in range(HH):
                buf = wpr1 if (mi * HH + n) % 2 == 0 else wpr2
                bufv = buf.rearrange("p (fc i) -> p fc i", fc=16, i=128)
                P.do("sync", lambda e, wdv=wdv, n=n, bufv=bufv: e.dma_start(
                    out=bufv, in_=wdv[:, :, n * 128:(n + 1) * 128]),
                    wait=(nowv("t"),), inc=(("dls", 16),))
                pmm = pA if n % 2 == 0 else pB
                for f in range(16):
                    P.do("tensor", lambda t, f=f, buf=buf, pmm=pmm: t.matmul(
                        pmm, buf[:, f * 128:(f + 1) * 128], xlnw[:, f * WIN:(f + 1) * WIN],
                        start=(f == 0), stop=(f == 15)),
                         wait=(nowv("dls"), nowv("s"), nowv("v")) if f == 0 else (),
                         inc=(("t", 1),) if f == 15 else ())
                if mk in "qkv":
                    mi3 = "qkv".index(mk)
                    hidx = mi3 * 8 + n
                    cw = convw.rearrange("p (m n j) -> p m n j", m=3, n=8, j=4)
                    cb = convb.rearrange("p (m n) -> p m n", m=3, n=8)
                    P.do("vector", lambda v, pmm=pmm: v.tensor_copy(out=prbuf[:, 3:WIN + 3], in_=pmm),
                         wait=(nowv("t"),))
                    P.do("vector", lambda v, hidx=hidx: v.tensor_copy(
                        out=prbuf[:, 0:3], in_=halos[:, hidx * 3:(hidx + 1) * 3]))
                    P.do("vector", lambda v: v.drain())
                    P.do("vector", lambda v, hidx=hidx: v.tensor_copy(
                        out=halos[:, hidx * 3:(hidx + 1) * 3], in_=prbuf[:, WIN:WIN + 3]))
                    P.do("vector", lambda v, mi3=mi3, n=n, cw=cw: v.tensor_scalar(
                        out=ktmp, in0=prbuf[:, 0:WIN], scalar1=cw[:, mi3, n, 0:1], scalar2=None,
                        op0=ALU.mult))
                    for jj in range(1, 4):
                        P.do("vector", lambda v, jj=jj, mi3=mi3, n=n, cw=cw: v.scalar_tensor_tensor(
                            out=ktmp, in0=prbuf[:, jj:WIN + jj], scalar=cw[:, mi3, n, jj:jj + 1],
                            in1=ktmp, op0=ALU.mult, op1=ALU.add))
                    P.do("vector", lambda v: v.drain(), inc=(("v", 1),))
                    if mk == "k":
                        P.do("scalar", lambda s, mi3=mi3, n=n, cb=cb: s.activation(
                            out=ktmp, in_=ktmp, func=AF.Silu, bias=cb[:, mi3, n:n + 1], scale=1.0),
                             wait=(nowv("v"),), inc=(("s", 1),))
                        P.do("vector", lambda v, n=n, kw4=kw4: v.tensor_scalar(
                            out=kw4[:, :, n, :], in0=ktmp.rearrange("p (c t) -> p c t", c=CPW, t=CK),
                            scalar1=SCALE, scalar2=None, op0=ALU.mult), wait=(nowv("s"),),
                             inc=(("v", 1),))
                    else:
                        d4 = qw4 if mk == "q" else vw4_
                        P.do("scalar", lambda s, d4=d4, mi3=mi3, n=n, cb=cb: s.activation(
                            out=d4[:, :, n, :], in_=ktmp.rearrange("p (c t) -> p c t", c=CPW, t=CK),
                            func=AF.Silu, bias=cb[:, mi3, n:n + 1], scale=1.0),
                             wait=(nowv("v"),), inc=(("s", 1),))
                elif mk == "a":
                    P.do("scalar", lambda s, n=n, pmm=pmm, t1e4=t1e4: s.activation(
                        out=t1e4[:, :, n, :], in_=pmm.rearrange("p (c t) -> p c t", c=CPW, t=CK),
                        func=AF.Exp, bias=negba[:, n:n + 1], scale=-1.0),
                         wait=(nowv("t"),), inc=(("s", 1),))
                else:
                    P.do("scalar", lambda s, n=n, pmm=pmm, gw4=gw4: s.activation(
                        out=gw4[:, :, n, :], in_=pmm.rearrange("p (c t) -> p c t", c=CPW, t=CK),
                        func=AF.Sigmoid), wait=(nowv("t"),), inc=(("s", 1),))
        P.do("vector", lambda v: v.tensor_scalar(out=t1ew, in0=t1ew, scalar1=1.0, scalar2=None,
                                                 op0=ALU.add), wait=(nowv("s"),), inc=(("v", 1),))
        P.do("scalar", lambda s: s.activation(out=nlaw, in_=t1ew, func=AF.Ln),
             wait=(nowv("v"),), inc=(("s", 1),))
        for f in range(16):
            P.do("tensor", lambda t, f=f: t.matmul(pC[0:8, :], wbsb[:, f * 8:(f + 1) * 8],
                                                   xlnw[:, f * WIN:(f + 1) * WIN],
                                                   start=(f == 0), stop=(f == 15)),
                 wait=(nowv("v"), nowv("s")) if f == 0 else (),
                 inc=(("t", 1),) if f == 15 else ())
        P.do("scalar", lambda s: s.activation(out=betacm[0:8, :], in_=pC[0:8, :], func=AF.Sigmoid,
                                              bias=bbcol[0:8, 0:1], scale=1.0),
             wait=(nowv("t"),), inc=(("s", 1),))

        for cl in range(CPW if K_WIN_MODE in ("full", "noout") else 0):
            ccol = slice(cl * 128, (cl + 1) * 128)
            kslice, qslice = kwv[:, ccol], qwv[:, ccol]
            for src, dst in ((nlaw, nlas), (t1ew, t1es), (vwv, vs_), (gww, gs_), (kwv, ks_scan)):
                P.do("tensor", lambda t, src=src, cc=ccol: t.transpose(pT[:, 0:128], src[:, cc],
                                                                       consts["id_bf"]),
                     wait=(nowv("v"), nowv("s")), inc=(("t", 1),))
                if dst is vs_:
                    P.do("scalar", lambda s, dst=dst: s.copy(out=vt_, in_=pT[:, 0:128]),
                         wait=(nowv("t"),), inc=(("s", 1),))
                else:
                    P.do("scalar", lambda s, dst=dst: s.copy(out=dst, in_=pT[:, 0:128]),
                         wait=(nowv("t"),), inc=(("s", 1),))
            if K_SCAN_STEPS < 2:
                continue
            P.do("tensor", lambda t, cl=cl: t.matmul(pC[:, 0:CK], consts["headsel"][0:8, :],
                                                     betacm[0:8, cl * CK:(cl + 1) * CK],
                                                     start=True, stop=True),
                 wait=(nowv("s"),), inc=(("t", 1),))
            P.do("vector", lambda v: v.tensor_tensor(out=bsc1[:, 0:CK], in0=pC[:, 0:CK],
                                                     in1=consts["tmask"], op=ALU.mult),
                 wait=(nowv("t"),))
            P.do("vector", lambda v: v.drain())
            P.do("vector", lambda v: v.tensor_reduce(out=betacol[:, 0:1], in_=bsc1[:, 0:CK], axis=AX.X,
                                                     op=ALU.add))
            P.do("vector", lambda v: v.drain())
            P.do("vector", lambda v: v.tensor_scalar(out=cs_, in0=t1es, scalar1=betacol[:, 0:1],
                                                     scalar2=None, op0=ALU.mult), inc=(("v", 1),))
            P.do("tensor", lambda t, kslice=kslice: t.matmul(pC[:, 0:128], kslice, kslice,
                                                             start=True, stop=True),
                 wait=(nowv("v"),), inc=(("t", 1),))
            P.do("vector", lambda v: v.tensor_tensor(out=gm_s, in0=pC[:, 0:128], in1=consts["cm_excl"],
                                                     op=ALU.mult), wait=(nowv("t"),), inc=(("v", 1),))
            P.do("tensor", lambda t, kslice=kslice, qslice=qslice: t.matmul(pD[:, 0:128], kslice, qslice,
                                                                            start=True, stop=True),
                 inc=(("t", 1),))
            P.do("vector", lambda v: v.tensor_tensor(out=gqk_s, in0=pD[:, 0:128], in1=consts["cm_incl"],
                                                     op=ALU.mult), wait=(nowv("t"),), inc=(("v", 1),))
            if K_SCAN_STEPS < 3:
                continue
            P.do("tensor", lambda t: t.matmul(pC[:, 0:128], consts["cm_excl"], nlas,
                                              start=True, stop=True), wait=(nowv("v"),), inc=(("t", 1),))
            P.do("scalar", lambda s: s.activation(out=rash, in_=pC[:, 0:128], func=AF.Exp, scale=1.0),
                 wait=(nowv("t"),), inc=(("s", 1),))
            P.do("tensor", lambda t: t.matmul(pC[:, 0:128], consts["cm_incl"], nlas,
                                              start=True, stop=True), wait=(nowv("s"),), inc=(("t", 1),))
            P.do("scalar", lambda s: s.activation(out=ahat, in_=pC[:, 0:128], func=AF.Exp, scale=-1.0),
                 wait=(nowv("t"),), inc=(("s", 1),))
            if K_SCAN_STEPS < 4:
                continue
            P.do("vector", lambda v, kslice=kslice: v.tensor_tensor(
                out=ks8.rearrange("p (n c) -> p n c", n=HH, c=128),
                in0=kslice.rearrange("p (one c) -> p one c", one=1).to_broadcast([128, HH, 128]),
                in1=consts["kmask8"].rearrange("p (n c) -> p n c", n=HH, c=128), op=ALU.mult))
            P.do("vector", lambda v, qslice=qslice: v.tensor_tensor(
                out=qs8.rearrange("p (n c) -> p n c", n=HH, c=128),
                in0=qslice.rearrange("p (one c) -> p one c", one=1).to_broadcast([128, HH, 128]),
                in1=consts["kmask8"].rearrange("p (n c) -> p n c", n=HH, c=128), op=ALU.mult),
                 inc=(("v", 1),))
            for n in range(HH):
                P.do("tensor", lambda t, n=n: t.matmul(
                    pC[:, 0:128], Sq[:, n * 128:(n + 1) * 128],
                    ks8[:, n * 128:(n + 1) * 128], start=(n == 0), stop=(n == 7)),
                     wait=(nowv("s"), nowv("v")) if n == 0 else (),
                     inc=(("t", 1),) if n == 7 else ())
            P.do("vector", lambda v: v.tensor_copy(out=mts, in_=pC[:, 0:128]), wait=(nowv("t"),),
                 inc=(("v", 1),))
            for n in range(HH):
                P.do("tensor", lambda t, n=n: t.matmul(
                    pD[:, 0:128], Sq[:, n * 128:(n + 1) * 128],
                    qs8[:, n * 128:(n + 1) * 128], start=(n == 0), stop=(n == 7)),
                     wait=(nowv("v"),) if n == 0 else (),
                     inc=(("t", 1),) if n == 7 else ())
            P.do("vector", lambda v: v.tensor_copy(out=p1ts, in_=pD[:, 0:128]), wait=(nowv("t"),),
                 inc=(("v", 1),))
            P.do("tensor", lambda t: t.transpose(pT[:, 0:128], mts, consts["id_bf"]),
                 wait=(nowv("v"),), inc=(("t", 1),))
            P.do("scalar", lambda s: s.copy(out=ms_, in_=pT[:, 0:128]), wait=(nowv("t"),), inc=(("s", 1),))
            P.do("tensor", lambda t: t.transpose(pT[:, 0:128], p1ts, consts["id_bf"]),
                 wait=(nowv("s"),), inc=(("t", 1),))
            P.do("scalar", lambda s: s.copy(out=p1s_, in_=pT[:, 0:128]), wait=(nowv("t"),), inc=(("s", 1),))
            P.do("vector", lambda v: v.tensor_tensor(out=vt_, in0=vt_, in1=rash, op=ALU.mult),
                 wait=(nowv("s"),))
            P.do("vector", lambda v: v.drain())
            P.do("vector", lambda v: v.scalar_tensor_tensor(out=rt_, in0=ms_, scalar=-1.0, in1=vt_,
                                                            op0=ALU.mult, op1=ALU.add))
            P.do("vector", lambda v: v.drain())
            P.do("vector", lambda v: v.tensor_tensor(out=uu, in0=rt_, in1=cs_, op=ALU.mult),
                 inc=(("v", 1),))
            if K_SCAN_STEPS < 5:
                continue
            for it in range(CK - 1):
                P.do("tensor", lambda t: t.matmul(pH[:, 0:128], gm_s, uu, start=True, stop=True),
                     wait=(nowv("v"),), inc=(("t", 1),))
                P.do("vector", lambda v: v.scalar_tensor_tensor(out=yy, in0=pH[:, 0:128], scalar=-1.0,
                                                                in1=rt_, op0=ALU.mult, op1=ALU.add),
                     wait=(nowv("t"),))
                P.do("vector", lambda v: v.drain())
                P.do("vector", lambda v: v.tensor_tensor(out=uu, in0=yy, in1=cs_, op=ALU.mult),
                     inc=(("v", 1),))
            if K_SCAN_STEPS < 6:
                continue
            P.do("tensor", lambda t: t.matmul(pD[:, 0:128], gqk_s, uu, start=True, stop=True),
                 wait=(nowv("v"),), inc=(("t", 1),))
            P.do("vector", lambda v: v.tensor_tensor(out=olnt, in0=pD[:, 0:128], in1=p1s_, op=ALU.add),
                 wait=(nowv("t"),))
            P.do("vector", lambda v: v.drain())
            P.do("vector", lambda v: v.tensor_tensor(out=oraw, in0=olnt, in1=ahat, op=ALU.mult),
                 inc=(("v", 1),))
            P.do("vector", lambda v: v.tensor_reduce(out=stat1[:, 0:1], in_=oraw, axis=AX.X, op=ALU.add))
            P.do("vector", lambda v: v.scalar_tensor_tensor(out=o2t, in0=oraw, scalar=1.0, in1=oraw,
                                                            op0=ALU.mult, op1=ALU.mult,
                                                            accum_out=stat2[:, 0:1]))
            P.do("vector", lambda v: v.drain())
            P.do("vector", lambda v: v.tensor_scalar(out=stat1[:, 0:1], in0=stat1[:, 0:1],
                                                     scalar1=1.0 / DH, scalar2=None, op0=ALU.mult))
            P.do("vector", lambda v: v.tensor_scalar(out=stat2[:, 0:1], in0=stat2[:, 0:1],
                                                     scalar1=1.0 / DH, scalar2=None, op0=ALU.mult))
            P.do("vector", lambda v: v.drain())
            P.do("vector", lambda v: v.tensor_scalar(out=stat3[:, 0:1], in0=stat1[:, 0:1],
                                                     scalar1=stat1[:, 0:1], scalar2=None, op0=ALU.mult))
            P.do("vector", lambda v: v.drain())
            P.do("vector", lambda v: v.tensor_tensor(out=stat2[:, 0:1], in0=stat2[:, 0:1],
                                                     in1=stat3[:, 0:1], op=ALU.subtract), inc=(("v", 1),))
            P.do("scalar", lambda s: s.activation(out=stat4[:, 0:1], in_=stat2[:, 0:1], func=AF.Ln,
                                                  bias=epscol[:, 0:1], scale=1.0), wait=(nowv("v"),),
                 inc=(("s", 1),))
            P.do("vector", lambda v: v.tensor_scalar(out=stat4[:, 0:1], in0=stat4[:, 0:1], scalar1=-0.5,
                                                     scalar2=None, op0=ALU.mult), wait=(nowv("s"),),
                 inc=(("v", 1),))
            P.do("scalar", lambda s: s.activation(out=stat3[:, 0:1], in_=stat4[:, 0:1], func=AF.Exp),
                 wait=(nowv("v"),), inc=(("s", 1),))
            P.do("vector", lambda v: v.drain(), wait=(nowv("s"),))
            P.do("vector", lambda v: v.tensor_scalar(out=olnt, in0=oraw, scalar1=stat1[:, 0:1],
                                                     scalar2=stat3[:, 0:1], op0=ALU.subtract,
                                                     op1=ALU.mult))
            P.do("vector", lambda v: v.drain())
            P.do("vector", lambda v: v.tensor_tensor(out=olnt, in0=olnt, in1=lnwbc, op=ALU.mult))
            P.do("vector", lambda v: v.drain())
            P.do("vector", lambda v: v.tensor_tensor(out=olnt, in0=olnt, in1=lnbbc, op=ALU.add))
            P.do("vector", lambda v: v.drain())
            P.do("vector", lambda v: v.tensor_tensor(out=ogs, in0=olnt, in1=gs_, op=ALU.mult),
                 inc=(("v", 1),))
            P.do("tensor", lambda t: t.transpose(pT[:, 0:128], ogs, consts["id_bf"]),
                 wait=(nowv("v"),), inc=(("t", 1),))
            ogv = ogw.rearrange("p (n c t) -> p n c t", n=HH, c=CPW, t=CK)
            P.do("scalar", lambda s, cl=cl, ogv=ogv: s.copy(
                out=ogv[:, :, cl, :], in_=pT[:, 0:128].rearrange("p (n t) -> p n t", n=HH, t=CK)),
                 wait=(nowv("t"),), inc=(("s", 1),))
            if K_SCAN_STEPS < 7:
                continue
            P.do("vector", lambda v: v.tensor_tensor(
                out=ah8.rearrange("p (n c) -> p n c", n=HH, c=128),
                in0=ahat.rearrange("p (one c) -> p one c", one=1).to_broadcast([128, HH, 128]),
                in1=consts["selmask"].rearrange("p (n c) -> p n c", n=HH, c=128), op=ALU.mult),
                 wait=(nowv("s"),), inc=(("v", 1),))
            P.do("tensor", lambda t: t.matmul(pS[:, 0:512], consts["ones128"], ah8[:, 0:512],
                                              start=True, stop=True), wait=(nowv("v"),))
            P.do("tensor", lambda t: t.matmul(pS[:, 512:1024], consts["ones128"], ah8[:, 512:1024],
                                              start=True, stop=True), inc=(("t", 1),))
            P.do("vector", lambda v: v.tensor_copy(out=acbc[:, 0:512], in_=pS[:, 0:512]),
                 wait=(nowv("t"),))
            P.do("vector", lambda v: v.tensor_copy(out=acbc[:, 512:1024], in_=pS[:, 512:1024]))
            P.do("vector", lambda v: v.tensor_tensor(
                out=wdiag.rearrange("p (n d) -> p n d", n=HH, d=128),
                in0=uu.rearrange("p (one d) -> p one d", one=1).to_broadcast([128, HH, 128]),
                in1=consts["bd_mask"].rearrange("p (n d) -> p n d", n=HH, d=128),
                op=ALU.mult), inc=(("v", 1),))
            P.do("tensor", lambda t: t.matmul(pS[:, 0:512], consts["id_bf"], Sq[:, 0:512],
                                              start=True, stop=False), wait=(nowv("v"),))
            P.do("tensor", lambda t: t.matmul(pS[:, 512:1024], consts["id_bf"], Sq[:, 512:1024],
                                              start=True, stop=False))
            P.do("tensor", lambda t: t.matmul(pS[:, 0:512], ks_scan, wdiag[:, 0:512],
                                              start=False, stop=True))
            P.do("tensor", lambda t: t.matmul(pS[:, 512:1024], ks_scan, wdiag[:, 512:1024],
                                              start=False, stop=True), inc=(("t", 1),))
            P.do("vector", lambda v: v.tensor_tensor(out=Sq[:, 0:512], in0=pS[:, 0:512],
                                                     in1=acbc[:, 0:512], op=ALU.mult),
                 wait=(nowv("t"),))
            P.do("vector", lambda v: v.tensor_tensor(out=Sq[:, 512:1024], in0=pS[:, 512:1024],
                                                     in1=acbc[:, 512:1024], op=ALU.mult),
                 inc=(("v", 1),))
        for r in range(4 if K_WIN_MODE in ("full",) else 0):
            for n in range(HH):
                buf = wpr1 if n % 2 == 0 else wpr2
                P.do("sync", lambda e, n=n, buf=buf: e.dma_start(
                    out=buf, in_=awfm_o[n * 128:(n + 1) * 128, :]),
                    wait=(nowv("t"),), inc=(("dls", 16),))
                for e4 in range(4):
                    pb = (pA, pB, pC, pD)[e4]
                    P.do("tensor", lambda t, r=r, n=n, e4=e4, pb=pb, buf=buf: t.matmul(
                        pb, ogw[:, n * WIN + r * 128:n * WIN + (r + 1) * 128],
                        buf[:, e4 * 512:(e4 + 1) * 512], start=(n == 0), stop=(n == 7)),
                         wait=(nowv("dls"), nowv("v"), nowv("s")) if (n == 0 and e4 == 0) else
                         ((nowv("dls"),) if e4 == 0 else ()),
                         inc=(("t", 1),) if e4 == 3 else ())
            for e4 in range(4):
                pb = (pA, pB, pC, pD)[e4]
                P.do("vector", lambda v, e4=e4, pb=pb: v.tensor_copy(
                    out=optile[:, e4 * 512:(e4 + 1) * 512], in_=pb),
                     wait=(nowv("t"),) if e4 == 0 else ())
            P.do("vector", lambda v: v.drain(), inc=(("v", 1),))
            P.do("sync", lambda e, w=w, r=r: e.dma_start(
                out=rs_in[w * WIN + r * 128:w * WIN + (r + 1) * 128, :], in_=optile),
                wait=(nowv("v"),), inc=(("dst", 16),))
        P.bar(f"w{w}")

    def phase_exchange2():
        P.do("gpsimd", lambda g: g.collective_compute(
            "ReduceScatter", ALU.add, replica_groups=[[0, 1], [2, 3], [4, 5], [6, 7]],
            ins=[rs_in[:, :]], outs=[rs_out[:, :]]), inc=(("cc", 1),))
        P.do("gpsimd", lambda g: None, wait=(nowv("cc"),))
        P.bar("e2")
        for i in range(16):
            P.do("sync", lambda e, i=i: e.dma_start(out=sq_dump, in_=x1bf[i * 128:(i + 1) * 128, :]),
                 wait=(nowv("v"),), inc=(("dls", 16),))
            P.do("sync", lambda e, i=i: e.dma_start(out=x2tile, in_=rs_out[i * 128:(i + 1) * 128, :]),
                 inc=(("dls", 16),))
            P.do("vector", lambda v: v.tensor_tensor(out=sq_dump, in0=sq_dump, in1=x2tile, op=ALU.add),
                 wait=(nowv("dls"),), inc=(("v", 1),))
            P.do("sync", lambda e, i=i: e.dma_start(out=x2bf[i * 128:(i + 1) * 128, :], in_=sq_dump),
                 wait=(nowv("v"),), inc=(("dst", 16),))
        P.bar("e2b")

    KPH = int(os.environ.get("K_PHASES", "99"))
    P.use("dls", "dls0")
    phase_weights()
    if KPH >= 2:
        P.use("dls", "dls1")
        phase_ffn(0)
    if KPH >= 3:
        phase_exchange1()
        attn_setup()
    if KPH >= 4:
        P.use("dls", "dls2")
        for w in range(min(NWIN, K_WINS)):
            if w == 4:
                P.use("dls", "dls3")
            attn_window(w)
    if KPH < 5:
        for i in range(16):
            P.do("sync", lambda e, i=i: e.dma_start(out=y_out[i * 128:(i + 1) * 128, :], in_=xt32),
                 wait=(P.now("v"),), inc=(("dst", 16),))
    if KPH >= 5 and DEBUG_DUMPS:
        for nm, src in (("rsin", rs_in), ("kw", kwv), ("qw", qwv), ("vw", vwv), ("nla", nlaw),
                        ("t1e", t1ew), ("gw", gww), ("og", ogw), ("Sq", Sq), ("xlnw", xlnw),
                        ("ahat", ahat), ("rash", rash), ("cs", cs_), ("gms", gm_s),
                        ("gqks", gqk_s), ("ms", ms_), ("p1s", p1s_), ("vt", vt_), ("rt", rt_),
                        ("uu", uu), ("oraw", oraw), ("ogs", ogs), ("ksscan", ks_scan),
                        ("acbc", acbc), ("betacol", betacol[:, 0:1].to_broadcast([128, 8]) if False else betacol[:, 0:1]), ("nlas", nlas),
                        ("lnwbc", lnwbc), ("lnbbc", lnbbc), ("olnt", olnt), ("gss", gs_),
                        ("lnrow", lnrow), ("onesc", consts["ones_1x128_f32"])):
            if src.dtype == F32:
                P.do("gpsimd", lambda g, nm=nm, src=src: g.dma_start(out=dbg[nm][:, :], in_=src),
                     inc=(("dlg", 16),))
            else:
                P.do("sync", lambda e, nm=nm, src=src: e.dma_start(out=dbg[nm][:, :], in_=src),
                     inc=(("dst", 16),))
        P.bar("dbgw")
    if KPH >= 5:
        phase_exchange2()
        P.use("dls", "dls4")
        phase_ffn(1)
    if KPH >= 5 and DEBUG_DUMPS:
        P.do("sync", lambda e: e.dma_start(out=dbg["x1"][:, :], in_=x1bf[:, :]), inc=(("dst", 16),))
        P.do("sync", lambda e: e.dma_start(out=dbg["ag"][:, :], in_=ag_out[:, :]), inc=(("dst", 16),))
        P.do("sync", lambda e: e.dma_start(out=dbg["rs"][:, :], in_=rs_out[:, :]), inc=(("dst", 16),))
        P.do("sync", lambda e: e.dma_start(out=dbg["x2"][:, :], in_=x2bf[:, :]), inc=(("dst", 16),))
    P.do("sync", lambda e: None, wait=(P.now("dst"),))

    with nc.Block() as block:
        @block.sync
        def _(e):
            P.replay("sync", e)

        @block.gpsimd
        def _(e):
            P.replay("gpsimd", e)

        @block.vector
        def _(e):
            P.replay("vector", e)

        @block.scalar
        def _(e):
            P.replay("scalar", e)

        @block.tensor
        def _(e):
            P.replay("tensor", e)

    stack.close()
    return nc


_NC_CACHE = None


def kernel(**inputs):
    global _NC_CACHE
    x = np.ascontiguousarray(np.asarray(inputs["L0_attn_x"], dtype=np.float32))
    B, TT, DD = x.shape
    down_idx = np.asarray(inputs["L0_ffn_down_reorder_index"])
    inv_down = np.argsort(down_idx)
    wd_eff = np.ascontiguousarray(np.asarray(inputs["L0_ffn_w_down"], np.float32)[:, inv_down])
    wg = np.ascontiguousarray(np.asarray(inputs["L0_ffn_w_gate"], np.float32))
    wu = np.ascontiguousarray(np.asarray(inputs["L0_ffn_w_up"], np.float32))
    A = lambda k: np.ascontiguousarray(np.asarray(inputs[f"L0_attn_{k}"], np.float32))
    Wq, Wk, Wv, Wa, Wb, Wg, Wo = A("Wq"), A("Wk"), A("Wv"), A("Wa"), A("Wb"), A("Wg"), A("Wo")
    ba, bb = A("ba"), A("bb")
    cw = {k: A(f"{k}conv_w") for k in "qkv"}
    cb = {k: A(f"{k}conv_b") for k in "qkv"}
    lnw, lnb = A("ln_w"), A("ln_b")

    in_maps = []
    for c in range(8):
        p, m = c // 2, c % 2
        hs = slice(m * HH * DH, (m + 1) * HH * DH)
        in_maps.append({
            "x0": np.ascontiguousarray(x[p, m * NT:(m + 1) * NT, :]),
            "ffn_wg": wg, "ffn_wu": wu, "ffn_wd": wd_eff,
            "aw_q": np.ascontiguousarray(Wq[hs]), "aw_k": np.ascontiguousarray(Wk[hs]),
            "aw_v": np.ascontiguousarray(Wv[hs]), "aw_a": np.ascontiguousarray(Wa[hs]),
            "aw_g": np.ascontiguousarray(Wg[hs]),
            "aw_b": np.ascontiguousarray(Wb[m * HH:(m + 1) * HH]),
            "aw_o": np.ascontiguousarray(Wo[:, hs]),
            "a_ba": np.ascontiguousarray(ba[hs]),
            "a_bb": np.ascontiguousarray(bb[m * HH:(m + 1) * HH]),
            "cw_q": np.ascontiguousarray(cw["q"][hs]), "cw_k": np.ascontiguousarray(cw["k"][hs]),
            "cw_v": np.ascontiguousarray(cw["v"][hs]),
            "cb_q": np.ascontiguousarray(cb["q"][hs]), "cb_k": np.ascontiguousarray(cb["k"][hs]),
            "cb_v": np.ascontiguousarray(cb["v"][hs]),
            "ln_w": lnw, "ln_b": lnb,
        })
    if _NC_CACHE is None:
        _NC_CACHE = build_nc()
    res = run_bass_kernel_spmd(_NC_CACHE, in_maps, core_ids=list(range(8)))
    out = np.empty((B, TT, DD), np.float32)
    for c in range(8):
        p, m = c // 2, c % 2
        out[p, m * NT:(m + 1) * NT, :] = res.results[c]["y"]
    return out


# version 07 (final)
# speedup vs baseline: 1.0678x; 1.0678x over previous
import os
import numpy as np
import ml_dtypes
import concourse.bass as bass
import concourse.mybir as mybir
from concourse.bass_utils import run_bass_kernel_spmd

F32 = mybir.dt.float32
BF16 = mybir.dt.bfloat16
FP8 = mybir.dt.float8e4
AF = mybir.ActivationFunctionType
ALU = mybir.AluOpType
AX = mybir.AxisListType

NT = 2048
T = 4096
D = 2048
HH = 8
DH = 128
CK = 16
WIN = 512
NWIN = T // WIN
CPW = WIN // CK
II = 5632
SCALE = float(DH) ** -0.5
LN_EPS = 1e-5
QS = 224.0

ENGINES = ("sync", "gpsimd", "vector", "scalar", "tensor")
DEBUG_DUMPS = os.environ.get("K_DEBUG_DUMPS") == "1"
K_WINS = int(os.environ.get("K_WINS", "99"))
K_WIN_MODE = os.environ.get("K_WIN_MODE", "full")
K_SCAN_STEPS = int(os.environ.get("K_SCAN_STEPS", "9"))


class PR:
    def __init__(self, nc):
        self.nc = nc
        self.ops = {e: [] for e in ENGINES}
        self.cnt = {}
        self.sems = {}
        self._sem_cms = []

    def sem(self, name):
        if name not in self.sems:
            cm = self.nc.semaphore(name)
            self.sems[name] = cm.__enter__()
            self._sem_cms.append(cm)
            self.cnt[name] = 0
        return self.sems[name]

    def use(self, name, real):
        self.sem(real)
        if not hasattr(self, "amap"):
            self.amap = {}
        self.amap[name] = real

    def _r(self, name):
        return getattr(self, "amap", {}).get(name, name)

    def do(self, eng, fn, wait=(), inc=()):
        wait = tuple((self._r(s), v) for s, v in wait)
        inc = tuple((self._r(s), n) for s, n in inc)
        for s, v in wait:
            assert v <= self.cnt.get(s, 0), f"wait on future {s}={v} (have {self.cnt.get(s, 0)})"
        self.ops[eng].append((tuple(w for w in wait if w[1] > 0), fn, tuple(inc)))
        for s, n in inc:
            self.cnt[s] = self.cnt.get(s, 0) + n

    def now(self, name):
        name = self._r(name)
        return (name, self.cnt.get(name, 0))

    def replay(self, eng, engobj):
        auto_drain = False
        for waits, fn, incs in self.ops[eng]:
            for s, v in waits:
                engobj.wait_ge(self.sems[s], v)
            if auto_drain:
                engobj.drain()
            r = fn(engobj)
            for s, n in incs:
                if r is not None:
                    r.then_inc(self.sems[s], n)
                    r = None
                else:
                    engobj.nop().then_inc(self.sems[s], n)

    def bar(self, tag):
        dma_all = [k for k in self.cnt if k.startswith(("dls", "dlg", "dst"))]
        dma_waits = tuple((s, self.cnt[s]) for s in dma_all if self.cnt[s] > 0)
        for e in ENGINES:
            self.do(e, lambda x: None, wait=dma_waits, inc=((f"bar_{e}", 1),))
        waits = tuple(self.now(f"bar_{e}") for e in ENGINES)
        for e in ENGINES:
            self.do(e, lambda x: None, wait=waits)


def mk_consts():
    c = {}
    c["id_f32"] = np.eye(128, dtype=np.float32)
    c["id_bf"] = np.eye(128).astype(ml_dtypes.bfloat16)
    c["ones_1x128_f32"] = np.ones((1, 128), dtype=np.float32)
    n_idx = np.arange(128) // CK
    t_idx = np.arange(128) % CK
    same = (n_idx[:, None] == n_idx[None, :])
    c["cm_incl"] = (same & (t_idx[:, None] <= t_idx[None, :])).astype(ml_dtypes.bfloat16)
    c["cm_excl"] = (same & (t_idx[:, None] < t_idx[None, :])).astype(ml_dtypes.bfloat16)
    bd = np.zeros((128, 1024), dtype=ml_dtypes.bfloat16)
    for n in range(HH):
        bd[n * CK:(n + 1) * CK, n * 128:(n + 1) * 128] = 1
    c["bd_mask"] = bd
    for n in range(HH):
        m = np.zeros((128, 128), dtype=ml_dtypes.bfloat16)
        m[n * CK + CK - 1, :] = 1
        c[f"sel_ac{n}"] = m
    hs = np.zeros((8, 128), dtype=ml_dtypes.bfloat16)
    for n in range(HH):
        hs[n, n * CK:(n + 1) * CK] = 1
    c["headsel"] = hs
    tm = np.zeros((128, CK), dtype=ml_dtypes.bfloat16)
    for p in range(128):
        tm[p, p % CK] = 1
    c["tmask"] = tm
    for b in range(16):
        m = np.zeros((16, 128), dtype=np.float32)
        m[b, :] = 1
        c[f"selr{b}"] = m
    km = np.zeros((128, 1024), dtype=ml_dtypes.bfloat16)
    for n in range(HH):
        km[:, n * 128 + n * CK:n * 128 + (n + 1) * CK] = 1
    c["kmask8"] = km
    sm = np.zeros((128, 1024), dtype=ml_dtypes.bfloat16)
    for n in range(HH):
        sm[n * CK + CK - 1, n * 128:(n + 1) * 128] = 1
    c["selmask"] = sm
    c["ones128"] = np.ones((128, 128), dtype=ml_dtypes.bfloat16)
    return c


def build_nc():
    nc = bass.Bass("TRN2", target_bir_lowering=False, debug=False)
    consts_np = mk_consts()
    Cdr = {k: nc.inline_tensor(np.ascontiguousarray(v), name=f"c_{k}").ap() for k, v in consts_np.items()}

    dt = nc.dram_tensor
    x0 = dt("x0", [NT, D], F32, kind="ExternalInput").ap()
    ffn_wg = dt("ffn_wg", [II, D], F32, kind="ExternalInput").ap()
    ffn_wu = dt("ffn_wu", [II, D], F32, kind="ExternalInput").ap()
    ffn_wd = dt("ffn_wd", [D, II], F32, kind="ExternalInput").ap()
    aw = {k: dt(f"aw_{k}", [HH * DH, D], F32, kind="ExternalInput").ap() for k in "qkvag"}
    aw_b = dt("aw_b", [HH, D], F32, kind="ExternalInput").ap()
    aw_o = dt("aw_o", [D, HH * DH], F32, kind="ExternalInput").ap()
    a_ba = dt("a_ba", [HH * DH], F32, kind="ExternalInput").ap()
    a_bb = dt("a_bb", [HH], F32, kind="ExternalInput").ap()
    cwt = {k: dt(f"cw_{k}", [HH * DH, 4], F32, kind="ExternalInput").ap() for k in "qkv"}
    cbt = {k: dt(f"cb_{k}", [HH * DH], F32, kind="ExternalInput").ap() for k in "qkv"}
    ln_w = dt("ln_w", [DH], F32, kind="ExternalInput").ap()
    ln_b = dt("ln_b", [DH], F32, kind="ExternalInput").ap()
    y_out = dt("y", [NT, D], F32, kind="ExternalOutput").ap()
    if DEBUG_DUMPS:
        dbg = {nm: dt(f"dbg_{nm}", shp, BF16, kind="ExternalOutput").ap()
               for nm, shp in (("x1", [NT, D]), ("ag", [T, D]), ("rs", [NT, D]), ("x2", [NT, D]),
                               ("rsin", [T, D]), ("kw", [128, 4096]), ("qw", [128, 4096]),
                               ("vw", [128, 4096]), ("nla", [128, 4096]), ("t1e", [128, 4096]),
                               ("gw", [128, 4096]), ("og", [128, 4096]), ("Sq", [128, 1024]),
                               ("xlnw", [128, 8192]),
                               ("ahat", [128, 128]), ("rash", [128, 128]), ("cs", [128, 128]),
                               ("gms", [128, 128]), ("gqks", [128, 128]), ("ms", [128, 128]),
                               ("p1s", [128, 128]), ("vt", [128, 128]), ("rt", [128, 128]),
                               ("uu", [128, 128]), ("oraw", [128, 128]), ("ogs", [128, 128]),
                               ("ksscan", [128, 128]), ("acbc", [128, 1024]), ("betacol", [128, 1]),
                               ("lnwbc", [128, 128]), ("lnbbc", [128, 128]), ("olnt", [128, 128]),
                               ("gss", [128, 128]), ("lnrow", [1, 256]), ("onesc", [1, 128]), ("lnwbc2", [128, 128]),
                               ("pa_mm", [128, 128]), ("lnrow2", [1, 256]), ("onesc2", [1, 128]),
                               ("nlas", [128, 128]))}

    x1bf = dt("x1bf", [NT, D], BF16, kind="Internal").ap()
    x2bf = dt("x2bf", [NT, D], BF16, kind="Internal").ap()
    ag_in = dt("ag_in", [NT, D], BF16, kind="Internal").ap()
    ag_out = dt("ag_out", [T, D], BF16, kind="Internal").ap()
    rs_in = dt("rs_in", [T, D], BF16, kind="Internal").ap()
    rs_out = dt("rs_out", [NT, D], BF16, kind="Internal").ap()
    wtmpA = dt("wtmpA", [II, D], BF16, kind="Internal").ap()
    wtmpB = dt("wtmpB", [D, II], BF16, kind="Internal").ap()
    wg_fm8 = dt("wg_fm8", [D, II], FP8, kind="Internal").ap()
    wu_fm8 = dt("wu_fm8", [D, II], FP8, kind="Internal").ap()
    wd_fm8 = dt("wd_fm8", [II, D], FP8, kind="Internal").ap()
    awfm = {k: dt(f"awfm_{k}", [D, HH * DH], BF16, kind="Internal").ap() for k in "qkvag"}
    awfm_o = dt("awfm_o", [HH * DH, D], BF16, kind="Internal").ap()

    P = PR(nc)
    for e in ENGINES:
        P.sem(f"bar_{e}")
    for s in ("dls", "dlg", "dst", "v", "s", "t", "cc"):
        P.sem(s)
    nowv = P.now

    import contextlib
    stack = contextlib.ExitStack()
    sb = lambda name, shape, dtp: stack.enter_context(nc.sbuf_tensor(name, shape, dtp))[:]
    ps = lambda name, shape, dtp: stack.enter_context(nc.psum_tensor(name, shape, dtp))[:]

    arena_bf = sb("arena_bf", [128, 24576], BF16)
    arena_f8 = sb("arena_f8", [128, 55296], FP8)
    xt32 = sb("xt32", [128, D], F32)
    xln32 = sb("xln32", [128, D], F32)
    sq_dump = sb("sq_dump", [128, D], BF16)
    x2tile = sb("x2tile", [128, D], BF16)
    qscl_pk = sb("qscl_pk", [128, 16], F32)
    rms_pk = sb("rms_pk", [128, 16], F32)
    tsq_pk = sb("tsq_pk", [128, 16], F32)
    tsd_pk = sb("tsd_pk", [128, 16], F32)
    row16 = sb("row16", [16, 128], F32)
    rmaxbc = sb("rmaxbc", [128, D], BF16)
    ddbc = sb("ddbc", [128, D], BF16)
    tqbc = sb("tqbc", [128, 512], BF16)
    wstr1 = sb("wstr1", [128, 2048], FP8)
    wstr2 = sb("wstr2", [128, 2048], FP8)
    wstr3 = sb("wstr3", [128, 512], FP8)
    wsc_g = sb("wsc_g", [128, 44], F32)
    wsc_u = sb("wsc_u", [128, 44], F32)
    wsc_d = sb("wsc_d", [128, 16], F32)
    consts = {}
    for k in consts_np:
        dd = BF16 if consts_np[k].dtype == ml_dtypes.bfloat16 else F32
        consts[k] = sb(f"k_{k}", list(consts_np[k].shape), dd)
    xlnw = sb("xlnw", [128, 16 * WIN], BF16)
    wpr1 = sb("wpr1", [128, 2048], BF16)
    wpr2 = sb("wpr2", [128, 2048], BF16)
    wbsb = sb("wbsb", [128, 128], BF16)
    convw = sb("convw", [128, 96], F32)
    convb = sb("convb", [128, 24], F32)
    negba = sb("negba", [128, 8], F32)
    bbcol = sb("bbcol", [8, 1], F32)
    lnwbc = sb("lnwbc", [128, 128], BF16)
    lnbbc = sb("lnbbc", [128, 128], BF16)
    lnrow = sb("lnrow", [1, 2 * DH], F32)
    halos = sb("halos", [128, 72], BF16)
    prbuf = sb("prbuf", [128, WIN + 3], BF16)
    ktmp = sb("ktmp", [128, WIN], BF16)
    betacm = sb("betacm", [8, WIN], BF16)
    Sq = sb("Sq", [128, 1024], BF16)
    wdiag = sb("wdiag", [128, 1024], BF16)
    acbc = sb("acbc", [128, 1024], BF16)
    nlas = sb("nlas", [128, 128], BF16)
    t1es = sb("t1es", [128, 128], BF16)
    cs_ = sb("cs_", [128, 128], BF16)
    vs_ = sb("vs_", [128, 128], BF16)
    gs_ = sb("gs_", [128, 128], BF16)
    ks_scan = sb("ks_scan", [128, 128], BF16)
    gm_s = sb("gm_s", [128, 128], BF16)
    gqk_s = sb("gqk_s", [128, 128], BF16)
    rash = sb("rash", [128, 128], BF16)
    ahat = sb("ahat", [128, 128], BF16)
    mts = sb("mts", [128, 128], BF16)
    p1ts = sb("p1ts", [128, 128], BF16)
    ms_ = sb("ms_", [128, 128], F32)
    p1s_ = sb("p1s_", [128, 128], F32)
    vt_ = sb("vt_", [128, 128], F32)
    rt_ = sb("rt_", [128, 128], F32)
    uu = sb("uu", [128, 128], BF16)
    yy = sb("yy", [128, 128], F32)
    oraw = sb("oraw", [128, 128], F32)
    olnt = sb("olnt", [128, 128], F32)
    o2t = sb("o2t", [128, 128], BF16)
    ogs = sb("ogs", [128, 128], BF16)
    betacol = sb("betacol", [128, 1], F32)
    bsc1 = sb("bsc1", [128, CK], F32)
    stat1 = sb("stat1", [128, 1], F32)
    stat2 = sb("stat2", [128, 1], F32)
    stat3 = sb("stat3", [128, 1], F32)
    stat4 = sb("stat4", [128, 1], F32)
    epscol = sb("epscol", [128, 1], F32)
    optile = sb("optile", [128, D], BF16)

    pA = ps("pA", [128, 512], F32)
    pB = ps("pB", [128, 512], F32)
    pC = ps("pC", [128, 512], F32)
    pD = ps("pD", [128, 512], F32)
    pH = ps("pH", [128, 512], F32)
    pT = ps("pT", [128, 1024], BF16)
    pS = ps("pS", [128, 1024], F32)

    tgrp = arena_bf[:, 0:22528]
    tmaxrun = arena_bf[:, 22528:23040]
    t1ew = arena_bf[:, 0:4096]
    gww = arena_bf[:, 4096:8192]
    ogw = arena_bf[:, 8192:12288]
    ks8 = arena_bf[:, 12288:13312]
    qs8 = arena_bf[:, 13312:14336]
    ah8 = arena_bf[:, 14336:15360]
    wide32 = arena_bf[:, 12288:23552].bitcast(F32)
    wbf_t = arena_bf[:, 0:5632]
    qxfm8 = arena_f8[:, 0:32768]
    qtg8 = arena_f8[:, 32768:55296]
    abf8 = arena_f8.bitcast(BF16)
    kwv = abf8[:, 0:4096]
    qwv = abf8[:, 4096:8192]
    vwv = abf8[:, 8192:12288]
    nlaw = abf8[:, 12288:16384]
    wq8t = arena_f8[:, 0:5632]

    vw4 = lambda t_: t_.rearrange("p (c n t) -> p c n t", c=CPW, n=HH, t=CK)

    def ln_tile(src_ap, ti, out32, src_is_bf):
        ld = "dlg" if src_is_bf else "dls"
        eng = "gpsimd" if src_is_bf else "sync"
        P.do(eng, lambda e, s=src_ap, i=ti: e.dma_start(out=xt32, in_=s[i * 128:(i + 1) * 128, :]),
             wait=(nowv("v"), nowv("s")), inc=((ld, 16),))
        P.do("vector", lambda v: v.tensor_reduce(out=stat1[:, 0:1], in_=xt32, axis=AX.X, op=ALU.add),
             wait=(nowv(ld),))
        P.do("vector", lambda v: v.scalar_tensor_tensor(out=sq_dump, in0=xt32, scalar=1.0, in1=xt32,
                                                        op0=ALU.mult, op1=ALU.mult,
                                                        accum_out=stat2[:, 0:1]))
        P.do("vector", lambda v: v.drain())
        P.do("vector", lambda v: v.tensor_scalar(out=stat1[:, 0:1], in0=stat1[:, 0:1], scalar1=1.0 / D,
                                                 scalar2=None, op0=ALU.mult))
        P.do("vector", lambda v: v.tensor_scalar(out=stat2[:, 0:1], in0=stat2[:, 0:1], scalar1=1.0 / D,
                                                 scalar2=None, op0=ALU.mult))
        P.do("vector", lambda v: v.drain())
        P.do("vector", lambda v: v.tensor_scalar(out=stat3[:, 0:1], in0=stat1[:, 0:1],
                                                 scalar1=stat1[:, 0:1], scalar2=None, op0=ALU.mult))
        P.do("vector", lambda v: v.drain())
        P.do("vector", lambda v: v.tensor_tensor(out=stat2[:, 0:1], in0=stat2[:, 0:1], in1=stat3[:, 0:1],
                                                 op=ALU.subtract), inc=(("v", 1),))
        P.do("scalar", lambda s: s.activation(out=stat4[:, 0:1], in_=stat2[:, 0:1], func=AF.Ln,
                                              bias=epscol[:, 0:1], scale=1.0), wait=(nowv("v"),), inc=(("s", 1),))
        P.do("vector", lambda v: v.tensor_scalar(out=stat4[:, 0:1], in0=stat4[:, 0:1], scalar1=-0.5,
                                                 scalar2=None, op0=ALU.mult), wait=(nowv("s"),),
             inc=(("v", 1),))
        P.do("scalar", lambda s: s.activation(out=stat3[:, 0:1], in_=stat4[:, 0:1], func=AF.Exp),
             wait=(nowv("v"),), inc=(("s", 1),))
        P.do("vector", lambda v: v.drain(), wait=(nowv("s"),))
        P.do("vector", lambda v, o=out32: v.tensor_scalar(out=o, in0=xt32, scalar1=stat1[:, 0:1],
                                                          scalar2=stat3[:, 0:1], op0=ALU.subtract,
                                                          op1=ALU.mult), inc=(("v", 1),))

    def prep_quant(nat_ap, rows, cols, qbf_dram, pk_sb):
        for i in range(rows // 128):
            P.do("sync", lambda e, i=i: e.dma_start(out=wide32[:, 0:cols],
                                                    in_=nat_ap[i * 128:(i + 1) * 128, :]),
                 wait=(nowv("v"), nowv("dst")), inc=(("dls", 16),))
            P.do("vector", lambda v: v.tensor_reduce(out=stat1[:, 0:1], in_=wide32[:, 0:cols], axis=AX.X,
                                                     op=ALU.max, apply_absolute_value=True),
                 wait=(nowv("dls"),))
            P.do("vector", lambda v: v.drain())
            P.do("vector", lambda v: v.reciprocal(out=stat2[:, 0:1], in_=stat1[:, 0:1]))
            P.do("vector", lambda v: v.drain())
            P.do("vector", lambda v: v.tensor_scalar(out=stat3[:, 0:1], in0=stat2[:, 0:1], scalar1=QS,
                                                     scalar2=None, op0=ALU.mult))
            P.do("vector", lambda v: v.tensor_scalar(out=stat4[:, 0:1], in0=stat1[:, 0:1],
                                                     scalar1=1.0 / QS, scalar2=None, op0=ALU.mult))
            P.do("vector", lambda v: v.drain(), inc=(("v", 1),))
            P.do("vector", lambda v, i=i: v.tensor_copy(out=pk_sb[:, i:i + 1], in_=stat4[:, 0:1]))
            P.do("vector", lambda v: v.tensor_scalar(out=wq8t[:, 0:cols], in0=wide32[:, 0:cols],
                                                     scalar1=stat3[:, 0:1], scalar2=None, op0=ALU.mult))
            P.do("vector", lambda v: v.drain())
            P.do("vector", lambda v: v.tensor_copy(out=wbf_t[:, 0:cols], in_=wq8t[:, 0:cols]),
                 inc=(("v", 1),))
            P.do("sync", lambda e, i=i: e.dma_start(out=qbf_dram[i * 128:(i + 1) * 128, :],
                                                    in_=wbf_t[:, 0:cols]),
                 wait=(nowv("v"),), inc=(("dst", 16),))
        P.bar("pq")

    def prep_transpose(qbf_dram, rows, cols, out_dram, out_fp8):
        for f in range(cols // 128):
            P.do("sync", lambda e, f=f: e.dma_start(out=wbf_t[:, 0:rows],
                                                    in_=qbf_dram[:, f * 128:(f + 1) * 128], transpose=True),
                 wait=(nowv("v"), nowv("dst")), inc=(("dls", 16),))
            if out_fp8:
                P.do("vector", lambda v: v.tensor_copy(out=wq8t[:, 0:rows], in_=wbf_t[:, 0:rows]),
                     wait=(nowv("dls"),), inc=(("v", 1),))
                P.do("sync", lambda e, f=f: e.dma_start(out=out_dram[f * 128:(f + 1) * 128, :],
                                                        in_=wq8t[:, 0:rows]),
                     wait=(nowv("v"),), inc=(("dst", 16),))
            else:
                P.do("vector", lambda v: v.drain(), wait=(nowv("dls"),), inc=(("v", 1),))
                P.do("sync", lambda e, f=f: e.dma_start(out=out_dram[f * 128:(f + 1) * 128, :],
                                                        in_=wbf_t[:, 0:rows]),
                     wait=(nowv("v"),), inc=(("dst", 16),))
        P.bar("pt")

    def prep_cast_bf(nat_ap, rows, cols, out_dram):
        for i in range((rows + 127) // 128):
            r = min(128, rows - i * 128)
            P.do("gpsimd", lambda g, i=i, r=r: g.dma_start(out=wbf_t[0:r, 0:cols],
                                                           in_=nat_ap[i * 128:i * 128 + r, :]),
                 wait=(nowv("dst"),), inc=(("dlg", 16),))
            P.do("sync", lambda e, i=i, r=r: e.dma_start(out=out_dram[i * 128:i * 128 + r, :],
                                                         in_=wbf_t[0:r, 0:cols]),
                 wait=(nowv("dlg"),), inc=(("dst", 16),))
        P.bar("pc")

    def phase_weights():
        P.do("gpsimd", lambda g: g.memset(epscol, 1e-5))
        P.bar("eps")
        for k in consts_np:
            P.do("sync", lambda e, k=k: e.dma_start(out=consts[k], in_=Cdr[k][:, :]), inc=(("dls", 16),))
        prep_quant(ffn_wg, II, D, wtmpA, wsc_g)
        prep_transpose(wtmpA, II, D, wg_fm8, True)
        prep_quant(ffn_wu, II, D, wtmpA, wsc_u)
        prep_transpose(wtmpA, II, D, wu_fm8, True)
        prep_quant(ffn_wd, D, II, wtmpB, wsc_d)
        prep_transpose(wtmpB, D, II, wd_fm8, True)
        for k in "qkvag":
            prep_cast_bf(aw[k], HH * DH, D, wtmpA[0:HH * DH, :])
            prep_transpose(wtmpA[0:HH * DH, :], HH * DH, D, awfm[k], False)
        prep_cast_bf(aw_o, D, HH * DH, wtmpB[0:D, 0:HH * DH])
        for f in range(HH):
            P.do("sync", lambda e, f=f: e.dma_start(out=wbf_t[:, 0:D],
                                                    in_=wtmpB[0:D, f * 128:(f + 1) * 128], transpose=True),
                 wait=(nowv("v"), nowv("dst")), inc=(("dls", 16),))
            P.do("vector", lambda v: v.drain(), wait=(nowv("dls"),), inc=(("v", 1),))
            P.do("sync", lambda e, f=f: e.dma_start(out=awfm_o[f * 128:(f + 1) * 128, :],
                                                    in_=wbf_t[:, 0:D]),
                 wait=(nowv("v"),), inc=(("dst", 16),))
        P.bar("pw")

    def phase_ffn(call):
        src = x0 if call == 0 else x2bf
        src_bf = (call == 1)
        qx4 = qxfm8.rearrange("p (f k) -> p f k", f=16, k=NT)
        for i in range(16):
            ln_tile(src, i, xln32, src_bf)
            P.do("vector", lambda v: v.tensor_reduce(out=stat1[:, 0:1], in_=xln32, axis=AX.X,
                                                     op=ALU.max, apply_absolute_value=True))
            P.do("vector", lambda v: v.drain())
            P.do("vector", lambda v: v.reciprocal(out=stat2[:, 0:1], in_=stat1[:, 0:1]))
            P.do("vector", lambda v: v.drain())
            P.do("vector", lambda v, i=i: v.tensor_scalar(out=qscl_pk[:, i:i + 1], in0=stat2[:, 0:1],
                                                          scalar1=QS, scalar2=None, op0=ALU.mult))
            P.do("vector", lambda v, i=i: v.tensor_scalar(out=rms_pk[:, i:i + 1], in0=stat1[:, 0:1],
                                                          scalar1=1.0 / QS, scalar2=None, op0=ALU.mult))
            P.do("vector", lambda v: v.drain())
            P.do("vector", lambda v, i=i: v.tensor_scalar(out=xln32, in0=xln32,
                                                          scalar1=qscl_pk[:, i:i + 1], scalar2=None,
                                                          op0=ALU.mult), inc=(("v", 1),))
            for f in range(16):
                P.do("tensor", lambda t, f=f: t.transpose(pH[:, 0:128], xln32[:, f * 128:(f + 1) * 128],
                                                          consts["id_f32"]),
                     wait=(nowv("v"), nowv("s")), inc=(("t", 1),))
                P.do("scalar", lambda s, f=f, i=i: s.copy(out=qx4[:, f, i * 128:(i + 1) * 128],
                                                          in_=pH[:, 0:128]),
                     wait=(nowv("t"),), inc=(("s", 1),))
        P.do("tensor", lambda t: t.transpose(pH[0:16, 0:128], rms_pk[:, 0:16], consts["id_f32"]),
             wait=(nowv("s"), nowv("v")), inc=(("t", 1),))
        P.do("vector", lambda v: v.tensor_copy(out=row16[0:16, 0:128], in_=pH[0:16, 0:128]),
             wait=(nowv("t"),), inc=(("v", 1),))
        for b in range(16):
            P.do("tensor", lambda t, b=b: t.matmul(pA[:, 0:128], consts[f"selr{b}"],
                                                   row16[0:16, 0:128], start=True, stop=True),
                 wait=(nowv("v"),), inc=(("t", 1),))
            P.do("vector", lambda v, b=b: v.tensor_copy(out=rmaxbc[:, b * 128:(b + 1) * 128],
                                                        in_=pA[:, 0:128]),
                 wait=(nowv("t"),), inc=(("v", 1),))
        P.do("tensor", lambda t: t.transpose(pH[0:16, 0:128], wsc_d[:, 0:16], consts["id_f32"]),
             wait=(nowv("v"),), inc=(("t", 1),))
        P.do("vector", lambda v: v.tensor_copy(out=row16[0:16, 0:128], in_=pH[0:16, 0:128]),
             wait=(nowv("t"),), inc=(("v", 1),))
        for b in range(16):
            P.do("tensor", lambda t, b=b: t.matmul(pA[:, 0:128], consts[f"selr{b}"],
                                                   row16[0:16, 0:128], start=True, stop=True),
                 wait=(nowv("v"),), inc=(("t", 1),))
            P.do("vector", lambda v, b=b: v.tensor_copy(out=ddbc[:, b * 128:(b + 1) * 128], in_=pA[:, 0:128]),
                 wait=(nowv("t"),), inc=(("v", 1),))
        P.bar(f"fA{call}")

        wgv = wg_fm8.rearrange("(fc p) i -> p fc i", p=128)
        wuv = wu_fm8.rearrange("(fc p) i -> p fc i", p=128)
        w1v = wstr1.rearrange("p (fc i) -> p fc i", fc=16, i=128)
        w2v = wstr2.rearrange("p (fc i) -> p fc i", fc=16, i=128)
        for tg in range(4):
            toks = slice(tg * 512, (tg + 1) * 512)
            for j in range(44):
                P.do("sync", lambda e, j=j: e.dma_start(out=w1v, in_=wgv[:, :, j * 128:(j + 1) * 128]),
                     wait=(nowv("t"),), inc=(("dls", 16),))
                P.do("sync", lambda e, j=j: e.dma_start(out=w2v, in_=wuv[:, :, j * 128:(j + 1) * 128]),
                     inc=(("dls", 16),))
                for f in range(16):
                    P.do("tensor",
                         lambda t, f=f, tk=toks: t.matmul(pA, wstr1[:, f * 128:(f + 1) * 128],
                                                          qx4[:, f, tk], start=(f == 0), stop=(f == 15)),
                         wait=(nowv("dls"), nowv("v")) if f == 0 else (),
                         inc=(("t", 1),) if f == 15 else ())
                for f in range(16):
                    P.do("tensor",
                         lambda t, f=f, tk=toks: t.matmul(pB, wstr2[:, f * 128:(f + 1) * 128],
                                                          qx4[:, f, tk], start=(f == 0), stop=(f == 15)),
                         inc=(("t", 1),) if f == 15 else ())
                P.do("vector", lambda v, j=j, tk=toks: v.scalar_tensor_tensor(
                    out=xln32[:, 0:512], in0=pA, scalar=wsc_g[:, j:j + 1], in1=rmaxbc[:, tk],
                    op0=ALU.mult, op1=ALU.mult), wait=(nowv("t"),), inc=(("v", 1),))
                P.do("scalar", lambda s: s.activation(out=xt32[:, 0:512], in_=xln32[:, 0:512],
                                                      func=AF.Silu), wait=(nowv("v"),), inc=(("s", 1),))
                P.do("vector", lambda v, j=j, tk=toks: v.scalar_tensor_tensor(
                    out=xln32[:, 512:1024], in0=pB, scalar=wsc_u[:, j:j + 1], in1=rmaxbc[:, tk],
                    op0=ALU.mult, op1=ALU.mult), inc=(("v", 1),))
                P.do("vector", lambda v: v.drain())
                P.do("vector", lambda v, j=j: v.tensor_tensor(out=tgrp[:, j * 512:(j + 1) * 512],
                                                              in0=xt32[:, 0:512], in1=xln32[:, 512:1024],
                                                              op=ALU.mult), wait=(nowv("s"),),
                     inc=(("v", 1),))
                P.do("scalar", lambda s, j=j: s.activation(out=xt32[:, 512:1024],
                                                           in_=tgrp[:, j * 512:(j + 1) * 512],
                                                           func=AF.Abs), wait=(nowv("v"),),
                     inc=(("s", 1),))
                if j == 0:
                    P.do("vector", lambda v: v.tensor_copy(out=tmaxrun, in_=xt32[:, 512:1024]),
                         wait=(nowv("s"),), inc=(("v", 1),))
                else:
                    P.do("vector", lambda v: v.tensor_tensor(out=tmaxrun, in0=tmaxrun,
                                                             in1=xt32[:, 512:1024], op=ALU.max),
                         wait=(nowv("s"),), inc=(("v", 1),))
            P.do("vector", lambda v: v.drain(), inc=(("v", 1),))
            for r in range(4):
                P.do("tensor", lambda t, r=r: t.transpose(pT[:, 0:128], tmaxrun[:, r * 128:(r + 1) * 128],
                                                          consts["id_bf"]),
                     wait=(nowv("v"),), inc=(("t", 1),))
                P.do("vector", lambda v, r=r: v.tensor_reduce(out=stat1[:, 0:1], in_=pT[:, 0:128],
                                                              axis=AX.X, op=ALU.max), wait=(nowv("t"),))
                P.do("vector", lambda v: v.drain())
                P.do("vector", lambda v: v.reciprocal(out=stat2[:, 0:1], in_=stat1[:, 0:1]))
                P.do("vector", lambda v: v.drain())
                P.do("vector", lambda v, r=r, tg=tg: v.tensor_scalar(
                    out=tsq_pk[:, tg * 4 + r:tg * 4 + r + 1], in0=stat2[:, 0:1], scalar1=QS,
                    scalar2=None, op0=ALU.mult))
                P.do("vector", lambda v, r=r, tg=tg: v.tensor_scalar(
                    out=tsd_pk[:, tg * 4 + r:tg * 4 + r + 1], in0=stat1[:, 0:1], scalar1=1.0 / QS,
                    scalar2=None, op0=ALU.mult), inc=(("v", 1),))
            P.do("tensor", lambda t, tg=tg: t.transpose(pH[0:4, 0:128], tsq_pk[:, tg * 4:tg * 4 + 4],
                                                        consts["id_f32"]),
                 wait=(nowv("v"),), inc=(("t", 1),))
            P.do("vector", lambda v: v.tensor_copy(out=row16[0:4, 0:128], in_=pH[0:4, 0:128]),
                 wait=(nowv("t"),), inc=(("v", 1),))
            for r in range(4):
                P.do("tensor", lambda t, r=r: t.matmul(pA[:, 0:128], consts[f"selr{r}"],
                                                       row16[0:16, 0:128], start=True, stop=True),
                     wait=(nowv("v"),), inc=(("t", 1),))
                P.do("vector", lambda v, r=r: v.tensor_copy(out=tqbc[:, r * 128:(r + 1) * 128],
                                                            in_=pA[:, 0:128]), wait=(nowv("t"),),
                     inc=(("v", 1),))
            P.do("vector", lambda v: v.drain())
            for j in range(44):
                P.do("vector", lambda v, j=j: v.tensor_tensor(out=qtg8[:, j * 512:(j + 1) * 512],
                                                              in0=tgrp[:, j * 512:(j + 1) * 512],
                                                              in1=tqbc, op=ALU.mult),
                     inc=(("v", 1),) if j == 43 else ())
            for e4 in range(4):
                ecols = slice(e4 * 512, (e4 + 1) * 512)
                for j in range(44):
                    P.do("sync", lambda e, j=j, ec=ecols: e.dma_start(
                        out=wstr3, in_=wd_fm8[j * 128:(j + 1) * 128, ec]),
                        wait=(nowv("t"),), inc=(("dls", 16),))
                    for r in range(4):
                        pb = (pA, pB, pC, pD)[r]
                        P.do("tensor", lambda t, j=j, r=r, pb=pb: t.matmul(
                            pb, qtg8[:, j * 512 + r * 128:j * 512 + (r + 1) * 128], wstr3,
                            start=(j == 0), stop=(j == 43)),
                             wait=(nowv("dls"), nowv("v")) if (j == 0 and r == 0) else
                             ((nowv("dls"),) if r == 0 else ()),
                             inc=(("t", 1),) if r == 3 else ())
                for r in range(4):
                    pb = (pA, pB, pC, pD)[r]
                    ti = tg * 4 + r
                    P.do("vector", lambda v, pb=pb, ti=ti, ec=ecols: v.scalar_tensor_tensor(
                        out=xln32[:, 0:512], in0=pb, scalar=tsd_pk[:, ti:ti + 1],
                        in1=ddbc[:, ec], op0=ALU.mult, op1=ALU.mult),
                         wait=(nowv("t"),))
                    if call == 0:
                        P.do("sync", lambda e, ti=ti, ec=ecols: e.dma_start(
                            out=xt32[:, 0:512], in_=x0[ti * 128:(ti + 1) * 128, ec]),
                            wait=(nowv("v"),), inc=(("dls", 16),))
                        P.do("vector", lambda v: v.drain(), wait=(nowv("dls"),))
                        P.do("vector", lambda v: v.scalar_tensor_tensor(
                            out=sq_dump[:, 0:512], in0=xln32[:, 0:512], scalar=0.5, in1=xt32[:, 0:512],
                            op0=ALU.mult, op1=ALU.add), inc=(("v", 1),))
                        P.do("sync", lambda e, ti=ti, ec=ecols: e.dma_start(
                            out=x1bf[ti * 128:(ti + 1) * 128, ec], in_=sq_dump[:, 0:512]),
                            wait=(nowv("v"),), inc=(("dst", 16),))
                    else:
                        P.do("gpsimd", lambda g, ti=ti, ec=ecols: g.dma_start(
                            out=xt32[:, 0:512], in_=x2bf[ti * 128:(ti + 1) * 128, ec]),
                            wait=(nowv("v"),), inc=(("dlg", 16),))
                        P.do("vector", lambda v: v.drain(), wait=(nowv("dlg"),))
                        P.do("vector", lambda v: v.scalar_tensor_tensor(
                            out=xln32[:, 512:1024], in0=xln32[:, 0:512], scalar=0.5, in1=xt32[:, 0:512],
                            op0=ALU.mult, op1=ALU.add), inc=(("v", 1),))
                        P.do("sync", lambda e, ti=ti, ec=ecols: e.dma_start(
                            out=y_out[ti * 128:(ti + 1) * 128, ec], in_=xln32[:, 512:1024]),
                            wait=(nowv("v"),), inc=(("dst", 16),))
        P.bar(f"fB{call}")

    def phase_exchange1():
        for i in range(16):
            ln_tile(x1bf, i, xln32, True)
            P.do("vector", lambda v: v.tensor_copy(out=sq_dump, in_=xln32), inc=(("v", 1),))
            P.do("sync", lambda e, i=i: e.dma_start(out=ag_in[i * 128:(i + 1) * 128, :], in_=sq_dump),
                 wait=(nowv("v"),), inc=(("dst", 16),))
        P.bar("e1a")
        P.do("gpsimd", lambda g: g.collective_compute(
            "AllGather", ALU.bypass, replica_groups=[[0, 1], [2, 3], [4, 5], [6, 7]],
            ins=[ag_in[:, :]], outs=[ag_out[:, :]]), inc=(("cc", 1),))
        P.do("gpsimd", lambda g: None, wait=(nowv("cc"),))
        P.bar("e1b")

    def attn_setup():
        P.do("gpsimd", lambda g: g.dma_start(out=sq_dump[0:8, :], in_=aw_b[:, :]), inc=(("dlg", 16),))
        for f in range(16):
            P.do("tensor", lambda t, f=f: t.transpose(pT[:, 0:8], sq_dump[0:8, f * 128:(f + 1) * 128],
                                                      consts["id_bf"][0:8, 0:8]),
                 wait=(nowv("dlg"), nowv("s")), inc=(("t", 1),))
            P.do("scalar", lambda s, f=f: s.copy(out=wbsb[:, f * 8:(f + 1) * 8], in_=pT[:, 0:8]),
                 wait=(nowv("t"),), inc=(("s", 1),))
        for mi, k in enumerate("qkv"):
            P.do("sync", lambda e, mi=mi, k=k: e.dma_start(
                out=convw.rearrange("p (m n j) -> p m n j", m=3, n=8, j=4)[:, mi, :, :],
                in_=cwt[k].rearrange("(n p) j -> p n j", p=128)), inc=(("dls", 16),))
            P.do("sync", lambda e, mi=mi, k=k: e.dma_start(
                out=convb.rearrange("p (m n) -> p m n", m=3, n=8)[:, mi, :],
                in_=cbt[k].rearrange("(n p) -> p n", p=128), allow_slow_non_contiguous=True),
                inc=(("dls", 16),))
        P.do("sync", lambda e: e.dma_start(out=negba, in_=a_ba.rearrange("(n p) -> p n", p=128),
                                           allow_slow_non_contiguous=True), inc=(("dls", 16),))
        P.do("vector", lambda v: v.tensor_scalar(out=negba, in0=negba, scalar1=-1.0, scalar2=None,
                                                 op0=ALU.mult), wait=(nowv("dls"),), inc=(("v", 1),))
        P.do("sync", lambda e: e.dma_start(out=bbcol[0:8, 0:1],
                                           in_=a_bb.rearrange("(n one) -> n one", one=1)),
             inc=(("dls", 16),))
        P.do("sync", lambda e: e.dma_start(out=lnrow[0:1, 0:DH], in_=ln_w.rearrange("(o d) -> o d", o=1)),
             inc=(("dls", 16),))
        P.do("sync", lambda e: e.dma_start(out=lnrow[0:1, DH:2 * DH],
                                           in_=ln_b.rearrange("(o d) -> o d", o=1)), inc=(("dls", 16),))
        if DEBUG_DUMPS:
            P.do("gpsimd", lambda g: g.dma_start(out=dbg["lnrow2"][:, :], in_=lnrow),
                 wait=(nowv("dls"),), inc=(("dlg", 16),))
            P.do("gpsimd", lambda g: g.dma_start(out=dbg["onesc2"][:, :], in_=consts["ones_1x128_f32"]),
                 inc=(("dlg", 16),))
        P.do("tensor", lambda t: t.matmul(pA[:, 0:128], consts["ones_1x128_f32"][0:1, :],
                                          lnrow[0:1, 0:DH], start=True, stop=True),
             wait=(nowv("dls"),), inc=(("t", 1),))
        P.do("vector", lambda v: v.tensor_copy(out=lnwbc, in_=pA[:, 0:128]), wait=(nowv("t"),),
             inc=(("v", 1),))
        if DEBUG_DUMPS:
            P.do("scalar", lambda sc: sc.copy(out=o2t, in_=pA[:, 0:128]), wait=(nowv("t"),),
                 inc=(("s", 1),))
            P.do("sync", lambda e: e.dma_start(out=dbg["pa_mm"][:, :], in_=o2t),
                 wait=(nowv("s"),), inc=(("dst", 16),))
        P.do("tensor", lambda t: t.matmul(pA[:, 0:128], consts["ones_1x128_f32"][0:1, :],
                                          lnrow[0:1, DH:2 * DH], start=True, stop=True),
             wait=(nowv("v"),), inc=(("t", 1),))
        P.do("vector", lambda v: v.tensor_copy(out=lnbbc, in_=pA[:, 0:128]), wait=(nowv("t"),),
             inc=(("v", 1),))
        if DEBUG_DUMPS:
            P.do("sync", lambda e: e.dma_start(out=dbg["lnwbc2"][:, :], in_=lnwbc),
                 wait=(nowv("v"),), inc=(("dst", 16),))
        P.do("gpsimd", lambda g: g.memset(Sq, 0.0))
        P.do("gpsimd", lambda g: g.memset(halos, 0.0))
        P.do("gpsimd", lambda g: g.memset(wdiag, 0.0))
        P.bar("as")

    def attn_window(w):
        kw4, qw4, vw4_, nla4, t1e4, gw4 = (vw4(t_) for t_ in (kwv, qwv, vwv, nlaw, t1ew, gww))
        for f in range(16):
            if K_WIN_MODE == "notrans":
                P.do("sync", lambda e, f=f, w=w: e.dma_start(
                    out=xlnw[:, f * WIN:(f + 1) * WIN],
                    in_=ag_out[w * 128 + f:w * 128 + f + 128, 0:WIN]),
                    inc=(("dls", 16),))
            else:
                P.do("sync", lambda e, f=f, w=w: e.dma_start(
                    out=xlnw[:, f * WIN:(f + 1) * WIN],
                    in_=ag_out[w * WIN:(w + 1) * WIN, f * 128:(f + 1) * 128], transpose=True),
                    inc=(("dls", 16),))
        mats = [("q", awfm["q"]), ("k", awfm["k"]), ("v", awfm["v"]), ("g", awfm["g"]), ("a", awfm["a"])]
        for mi, (mk, wdram) in enumerate(mats):
            wdv = wdram.rearrange("(fc p) i -> p fc i", p=128)
            for n in range(HH):
                buf = wpr1 if (mi * HH + n) % 2 == 0 else wpr2
                bufv = buf.rearrange("p (fc i) -> p fc i", fc=16, i=128)
                P.do("sync", lambda e, wdv=wdv, n=n, bufv=bufv: e.dma_start(
                    out=bufv, in_=wdv[:, :, n * 128:(n + 1) * 128]),
                    wait=(nowv("t"),), inc=(("dls", 16),))
                pmm = pA if n % 2 == 0 else pB
                for f in range(16):
                    P.do("tensor", lambda t, f=f, buf=buf, pmm=pmm: t.matmul(
                        pmm, buf[:, f * 128:(f + 1) * 128], xlnw[:, f * WIN:(f + 1) * WIN],
                        start=(f == 0), stop=(f == 15)),
                         wait=(nowv("dls"), nowv("s"), nowv("v")) if f == 0 else (),
                         inc=(("t", 1),) if f == 15 else ())
                if mk in "qkv":
                    mi3 = "qkv".index(mk)
                    hidx = mi3 * 8 + n
                    cw = convw.rearrange("p (m n j) -> p m n j", m=3, n=8, j=4)
                    cb = convb.rearrange("p (m n) -> p m n", m=3, n=8)
                    P.do("vector", lambda v, pmm=pmm: v.tensor_copy(out=prbuf[:, 3:WIN + 3], in_=pmm),
                         wait=(nowv("t"),))
                    P.do("vector", lambda v, hidx=hidx: v.tensor_copy(
                        out=prbuf[:, 0:3], in_=halos[:, hidx * 3:(hidx + 1) * 3]))
                    P.do("vector", lambda v: v.drain())
                    P.do("vector", lambda v, hidx=hidx: v.tensor_copy(
                        out=halos[:, hidx * 3:(hidx + 1) * 3], in_=prbuf[:, WIN:WIN + 3]))
                    P.do("vector", lambda v, mi3=mi3, n=n, cw=cw: v.tensor_scalar(
                        out=ktmp, in0=prbuf[:, 0:WIN], scalar1=cw[:, mi3, n, 0:1], scalar2=None,
                        op0=ALU.mult))
                    for jj in range(1, 4):
                        P.do("vector", lambda v: v.drain())
                        P.do("vector", lambda v, jj=jj, mi3=mi3, n=n, cw=cw: v.scalar_tensor_tensor(
                            out=ktmp, in0=prbuf[:, jj:WIN + jj], scalar=cw[:, mi3, n, jj:jj + 1],
                            in1=ktmp, op0=ALU.mult, op1=ALU.add))
                    P.do("vector", lambda v: v.drain(), inc=(("v", 1),))
                    if mk == "k":
                        P.do("scalar", lambda s, mi3=mi3, n=n, cb=cb: s.activation(
                            out=ktmp, in_=ktmp, func=AF.Silu, bias=cb[:, mi3, n:n + 1], scale=1.0),
                             wait=(nowv("v"),), inc=(("s", 1),))
                        P.do("vector", lambda v, n=n, kw4=kw4: v.tensor_scalar(
                            out=kw4[:, :, n, :], in0=ktmp.rearrange("p (c t) -> p c t", c=CPW, t=CK),
                            scalar1=SCALE, scalar2=None, op0=ALU.mult), wait=(nowv("s"),),
                             inc=(("v", 1),))
                    else:
                        d4 = qw4 if mk == "q" else vw4_
                        P.do("scalar", lambda s, d4=d4, mi3=mi3, n=n, cb=cb: s.activation(
                            out=d4[:, :, n, :], in_=ktmp.rearrange("p (c t) -> p c t", c=CPW, t=CK),
                            func=AF.Silu, bias=cb[:, mi3, n:n + 1], scale=1.0),
                             wait=(nowv("v"),), inc=(("s", 1),))
                elif mk == "a":
                    P.do("scalar", lambda s, n=n, pmm=pmm, t1e4=t1e4: s.activation(
                        out=t1e4[:, :, n, :], in_=pmm.rearrange("p (c t) -> p c t", c=CPW, t=CK),
                        func=AF.Exp, bias=negba[:, n:n + 1], scale=-1.0),
                         wait=(nowv("t"),), inc=(("s", 1),))
                else:
                    P.do("scalar", lambda s, n=n, pmm=pmm, gw4=gw4: s.activation(
                        out=gw4[:, :, n, :], in_=pmm.rearrange("p (c t) -> p c t", c=CPW, t=CK),
                        func=AF.Sigmoid), wait=(nowv("t"),), inc=(("s", 1),))
        P.do("vector", lambda v: v.tensor_scalar(out=t1ew, in0=t1ew, scalar1=1.0, scalar2=None,
                                                 op0=ALU.add), wait=(nowv("s"),), inc=(("v", 1),))
        P.do("scalar", lambda s: s.activation(out=nlaw, in_=t1ew, func=AF.Ln),
             wait=(nowv("v"),), inc=(("s", 1),))
        for f in range(16):
            P.do("tensor", lambda t, f=f: t.matmul(pC[0:8, :], wbsb[:, f * 8:(f + 1) * 8],
                                                   xlnw[:, f * WIN:(f + 1) * WIN],
                                                   start=(f == 0), stop=(f == 15)),
                 wait=(nowv("v"), nowv("s")) if f == 0 else (),
                 inc=(("t", 1),) if f == 15 else ())
        P.do("scalar", lambda s: s.activation(out=betacm[0:8, :], in_=pC[0:8, :], func=AF.Sigmoid,
                                              bias=bbcol[0:8, 0:1], scale=1.0),
             wait=(nowv("t"),), inc=(("s", 1),))

        for cl in range(CPW if K_WIN_MODE in ("full", "noout") else 0):
            ccol = slice(cl * 128, (cl + 1) * 128)
            kslice, qslice = kwv[:, ccol], qwv[:, ccol]
            for src, dst in ((nlaw, nlas), (t1ew, t1es), (vwv, vs_), (gww, gs_), (kwv, ks_scan)):
                P.do("tensor", lambda t, src=src, cc=ccol: t.transpose(pT[:, 0:128], src[:, cc],
                                                                       consts["id_bf"]),
                     wait=(nowv("v"), nowv("s")), inc=(("t", 1),))
                if dst is vs_:
                    P.do("scalar", lambda s, dst=dst: s.copy(out=vt_, in_=pT[:, 0:128]),
                         wait=(nowv("t"),), inc=(("s", 1),))
                else:
                    P.do("scalar", lambda s, dst=dst: s.copy(out=dst, in_=pT[:, 0:128]),
                         wait=(nowv("t"),), inc=(("s", 1),))
            if K_SCAN_STEPS < 2:
                continue
            P.do("tensor", lambda t, cl=cl: t.matmul(pC[:, 0:CK], consts["headsel"][0:8, :],
                                                     betacm[0:8, cl * CK:(cl + 1) * CK],
                                                     start=True, stop=True),
                 wait=(nowv("s"),), inc=(("t", 1),))
            P.do("vector", lambda v: v.tensor_tensor(out=bsc1[:, 0:CK], in0=pC[:, 0:CK],
                                                     in1=consts["tmask"], op=ALU.mult),
                 wait=(nowv("t"),))
            P.do("vector", lambda v: v.drain())
            P.do("vector", lambda v: v.tensor_reduce(out=betacol[:, 0:1], in_=bsc1[:, 0:CK], axis=AX.X,
                                                     op=ALU.add))
            P.do("vector", lambda v: v.drain())
            P.do("vector", lambda v: v.tensor_scalar(out=cs_, in0=t1es, scalar1=betacol[:, 0:1],
                                                     scalar2=None, op0=ALU.mult), inc=(("v", 1),))
            P.do("tensor", lambda t, kslice=kslice: t.matmul(pC[:, 0:128], kslice, kslice,
                                                             start=True, stop=True),
                 wait=(nowv("v"),), inc=(("t", 1),))
            P.do("vector", lambda v: v.tensor_tensor(out=gm_s, in0=pC[:, 0:128], in1=consts["cm_excl"],
                                                     op=ALU.mult), wait=(nowv("t"),), inc=(("v", 1),))
            P.do("tensor", lambda t, kslice=kslice, qslice=qslice: t.matmul(pD[:, 0:128], kslice, qslice,
                                                                            start=True, stop=True),
                 inc=(("t", 1),))
            P.do("vector", lambda v: v.tensor_tensor(out=gqk_s, in0=pD[:, 0:128], in1=consts["cm_incl"],
                                                     op=ALU.mult), wait=(nowv("t"),), inc=(("v", 1),))
            if K_SCAN_STEPS < 3:
                continue
            P.do("tensor", lambda t: t.matmul(pC[:, 0:128], consts["cm_excl"], nlas,
                                              start=True, stop=True), wait=(nowv("v"),), inc=(("t", 1),))
            P.do("scalar", lambda s: s.activation(out=rash, in_=pC[:, 0:128], func=AF.Exp, scale=1.0),
                 wait=(nowv("t"),), inc=(("s", 1),))
            P.do("tensor", lambda t: t.matmul(pC[:, 0:128], consts["cm_incl"], nlas,
                                              start=True, stop=True), wait=(nowv("s"),), inc=(("t", 1),))
            P.do("scalar", lambda s: s.activation(out=ahat, in_=pC[:, 0:128], func=AF.Exp, scale=-1.0),
                 wait=(nowv("t"),), inc=(("s", 1),))
            if K_SCAN_STEPS < 4:
                continue
            P.do("vector", lambda v, kslice=kslice: v.tensor_tensor(
                out=ks8.rearrange("p (n c) -> p n c", n=HH, c=128),
                in0=kslice.rearrange("p (one c) -> p one c", one=1).to_broadcast([128, HH, 128]),
                in1=consts["kmask8"].rearrange("p (n c) -> p n c", n=HH, c=128), op=ALU.mult))
            P.do("vector", lambda v, qslice=qslice: v.tensor_tensor(
                out=qs8.rearrange("p (n c) -> p n c", n=HH, c=128),
                in0=qslice.rearrange("p (one c) -> p one c", one=1).to_broadcast([128, HH, 128]),
                in1=consts["kmask8"].rearrange("p (n c) -> p n c", n=HH, c=128), op=ALU.mult),
                 inc=(("v", 1),))
            for n in range(HH):
                P.do("tensor", lambda t, n=n: t.matmul(
                    pC[:, 0:128], Sq[:, n * 128:(n + 1) * 128],
                    ks8[:, n * 128:(n + 1) * 128], start=(n == 0), stop=(n == 7)),
                     wait=(nowv("s"), nowv("v")) if n == 0 else (),
                     inc=(("t", 1),) if n == 7 else ())
            P.do("vector", lambda v: v.tensor_copy(out=mts, in_=pC[:, 0:128]), wait=(nowv("t"),),
                 inc=(("v", 1),))
            for n in range(HH):
                P.do("tensor", lambda t, n=n: t.matmul(
                    pD[:, 0:128], Sq[:, n * 128:(n + 1) * 128],
                    qs8[:, n * 128:(n + 1) * 128], start=(n == 0), stop=(n == 7)),
                     wait=(nowv("v"),) if n == 0 else (),
                     inc=(("t", 1),) if n == 7 else ())
            P.do("vector", lambda v: v.tensor_copy(out=p1ts, in_=pD[:, 0:128]), wait=(nowv("t"),),
                 inc=(("v", 1),))
            P.do("tensor", lambda t: t.transpose(pT[:, 0:128], mts, consts["id_bf"]),
                 wait=(nowv("v"),), inc=(("t", 1),))
            P.do("scalar", lambda s: s.copy(out=ms_, in_=pT[:, 0:128]), wait=(nowv("t"),), inc=(("s", 1),))
            P.do("tensor", lambda t: t.transpose(pT[:, 0:128], p1ts, consts["id_bf"]),
                 wait=(nowv("s"),), inc=(("t", 1),))
            P.do("scalar", lambda s: s.copy(out=p1s_, in_=pT[:, 0:128]), wait=(nowv("t"),), inc=(("s", 1),))
            P.do("vector", lambda v: v.tensor_tensor(out=vt_, in0=vt_, in1=rash, op=ALU.mult),
                 wait=(nowv("s"),))
            P.do("vector", lambda v: v.drain())
            P.do("vector", lambda v: v.scalar_tensor_tensor(out=rt_, in0=ms_, scalar=-1.0, in1=vt_,
                                                            op0=ALU.mult, op1=ALU.add))
            P.do("vector", lambda v: v.drain())
            P.do("vector", lambda v: v.tensor_tensor(out=uu, in0=rt_, in1=cs_, op=ALU.mult),
                 inc=(("v", 1),))
            if K_SCAN_STEPS < 5:
                continue
            for it in range(CK - 1):
                P.do("tensor", lambda t: t.matmul(pH[:, 0:128], gm_s, uu, start=True, stop=True),
                     wait=(nowv("v"),), inc=(("t", 1),))
                P.do("vector", lambda v: v.scalar_tensor_tensor(out=yy, in0=pH[:, 0:128], scalar=-1.0,
                                                                in1=rt_, op0=ALU.mult, op1=ALU.add),
                     wait=(nowv("t"),))
                P.do("vector", lambda v: v.drain())
                P.do("vector", lambda v: v.tensor_tensor(out=uu, in0=yy, in1=cs_, op=ALU.mult),
                     inc=(("v", 1),))
            if K_SCAN_STEPS < 6:
                continue
            P.do("tensor", lambda t: t.matmul(pD[:, 0:128], gqk_s, uu, start=True, stop=True),
                 wait=(nowv("v"),), inc=(("t", 1),))
            P.do("vector", lambda v: v.tensor_tensor(out=olnt, in0=pD[:, 0:128], in1=p1s_, op=ALU.add),
                 wait=(nowv("t"),))
            P.do("vector", lambda v: v.drain())
            P.do("vector", lambda v: v.tensor_tensor(out=oraw, in0=olnt, in1=ahat, op=ALU.mult),
                 inc=(("v", 1),))
            P.do("vector", lambda v: v.tensor_reduce(out=stat1[:, 0:1], in_=oraw, axis=AX.X, op=ALU.add))
            P.do("vector", lambda v: v.scalar_tensor_tensor(out=o2t, in0=oraw, scalar=1.0, in1=oraw,
                                                            op0=ALU.mult, op1=ALU.mult,
                                                            accum_out=stat2[:, 0:1]))
            P.do("vector", lambda v: v.drain())
            P.do("vector", lambda v: v.tensor_scalar(out=stat1[:, 0:1], in0=stat1[:, 0:1],
                                                     scalar1=1.0 / DH, scalar2=None, op0=ALU.mult))
            P.do("vector", lambda v: v.tensor_scalar(out=stat2[:, 0:1], in0=stat2[:, 0:1],
                                                     scalar1=1.0 / DH, scalar2=None, op0=ALU.mult))
            P.do("vector", lambda v: v.drain())
            P.do("vector", lambda v: v.tensor_scalar(out=stat3[:, 0:1], in0=stat1[:, 0:1],
                                                     scalar1=stat1[:, 0:1], scalar2=None, op0=ALU.mult))
            P.do("vector", lambda v: v.drain())
            P.do("vector", lambda v: v.tensor_tensor(out=stat2[:, 0:1], in0=stat2[:, 0:1],
                                                     in1=stat3[:, 0:1], op=ALU.subtract), inc=(("v", 1),))
            P.do("scalar", lambda s: s.activation(out=stat4[:, 0:1], in_=stat2[:, 0:1], func=AF.Ln,
                                                  bias=epscol[:, 0:1], scale=1.0), wait=(nowv("v"),),
                 inc=(("s", 1),))
            P.do("vector", lambda v: v.tensor_scalar(out=stat4[:, 0:1], in0=stat4[:, 0:1], scalar1=-0.5,
                                                     scalar2=None, op0=ALU.mult), wait=(nowv("s"),),
                 inc=(("v", 1),))
            P.do("scalar", lambda s: s.activation(out=stat3[:, 0:1], in_=stat4[:, 0:1], func=AF.Exp),
                 wait=(nowv("v"),), inc=(("s", 1),))
            P.do("vector", lambda v: v.drain(), wait=(nowv("s"),))
            P.do("vector", lambda v: v.tensor_scalar(out=olnt, in0=oraw, scalar1=stat1[:, 0:1],
                                                     scalar2=stat3[:, 0:1], op0=ALU.subtract,
                                                     op1=ALU.mult))
            P.do("vector", lambda v: v.drain())
            P.do("vector", lambda v: v.tensor_tensor(out=olnt, in0=olnt, in1=lnwbc, op=ALU.mult))
            P.do("vector", lambda v: v.drain())
            P.do("vector", lambda v: v.tensor_tensor(out=olnt, in0=olnt, in1=lnbbc, op=ALU.add))
            P.do("vector", lambda v: v.drain())
            P.do("vector", lambda v: v.tensor_tensor(out=ogs, in0=olnt, in1=gs_, op=ALU.mult),
                 inc=(("v", 1),))
            P.do("tensor", lambda t: t.transpose(pT[:, 0:128], ogs, consts["id_bf"]),
                 wait=(nowv("v"),), inc=(("t", 1),))
            ogv = ogw.rearrange("p (n c t) -> p n c t", n=HH, c=CPW, t=CK)
            P.do("scalar", lambda s, cl=cl, ogv=ogv: s.copy(
                out=ogv[:, :, cl, :], in_=pT[:, 0:128].rearrange("p (n t) -> p n t", n=HH, t=CK)),
                 wait=(nowv("t"),), inc=(("s", 1),))
            if K_SCAN_STEPS < 7:
                continue
            P.do("vector", lambda v: v.tensor_tensor(
                out=ah8.rearrange("p (n c) -> p n c", n=HH, c=128),
                in0=ahat.rearrange("p (one c) -> p one c", one=1).to_broadcast([128, HH, 128]),
                in1=consts["selmask"].rearrange("p (n c) -> p n c", n=HH, c=128), op=ALU.mult),
                 wait=(nowv("s"),), inc=(("v", 1),))
            P.do("tensor", lambda t: t.matmul(pS[:, 0:512], consts["ones128"], ah8[:, 0:512],
                                              start=True, stop=True), wait=(nowv("v"),))
            P.do("tensor", lambda t: t.matmul(pS[:, 512:1024], consts["ones128"], ah8[:, 512:1024],
                                              start=True, stop=True), inc=(("t", 1),))
            P.do("vector", lambda v: v.tensor_copy(out=acbc[:, 0:512], in_=pS[:, 0:512]),
                 wait=(nowv("t"),))
            P.do("vector", lambda v: v.tensor_copy(out=acbc[:, 512:1024], in_=pS[:, 512:1024]))
            P.do("vector", lambda v: v.tensor_tensor(
                out=wdiag.rearrange("p (n d) -> p n d", n=HH, d=128),
                in0=uu.rearrange("p (one d) -> p one d", one=1).to_broadcast([128, HH, 128]),
                in1=consts["bd_mask"].rearrange("p (n d) -> p n d", n=HH, d=128),
                op=ALU.mult), inc=(("v", 1),))
            P.do("tensor", lambda t: t.matmul(pS[:, 0:512], consts["id_bf"], Sq[:, 0:512],
                                              start=True, stop=False), wait=(nowv("v"),))
            P.do("tensor", lambda t: t.matmul(pS[:, 512:1024], consts["id_bf"], Sq[:, 512:1024],
                                              start=True, stop=False))
            P.do("tensor", lambda t: t.matmul(pS[:, 0:512], ks_scan, wdiag[:, 0:512],
                                              start=False, stop=True))
            P.do("tensor", lambda t: t.matmul(pS[:, 512:1024], ks_scan, wdiag[:, 512:1024],
                                              start=False, stop=True), inc=(("t", 1),))
            P.do("vector", lambda v: v.tensor_tensor(out=Sq[:, 0:512], in0=pS[:, 0:512],
                                                     in1=acbc[:, 0:512], op=ALU.mult),
                 wait=(nowv("t"),))
            P.do("vector", lambda v: v.tensor_tensor(out=Sq[:, 512:1024], in0=pS[:, 512:1024],
                                                     in1=acbc[:, 512:1024], op=ALU.mult),
                 inc=(("v", 1),))
        for r in range(4 if K_WIN_MODE in ("full",) else 0):
            for n in range(HH):
                buf = wpr1 if n % 2 == 0 else wpr2
                P.do("sync", lambda e, n=n, buf=buf: e.dma_start(
                    out=buf, in_=awfm_o[n * 128:(n + 1) * 128, :]),
                    wait=(nowv("t"),), inc=(("dls", 16),))
                for e4 in range(4):
                    pb = (pA, pB, pC, pD)[e4]
                    P.do("tensor", lambda t, r=r, n=n, e4=e4, pb=pb, buf=buf: t.matmul(
                        pb, ogw[:, n * WIN + r * 128:n * WIN + (r + 1) * 128],
                        buf[:, e4 * 512:(e4 + 1) * 512], start=(n == 0), stop=(n == 7)),
                         wait=(nowv("dls"), nowv("v"), nowv("s")) if (n == 0 and e4 == 0) else
                         ((nowv("dls"),) if e4 == 0 else ()),
                         inc=(("t", 1),) if e4 == 3 else ())
            for e4 in range(4):
                pb = (pA, pB, pC, pD)[e4]
                P.do("vector", lambda v, e4=e4, pb=pb: v.tensor_copy(
                    out=optile[:, e4 * 512:(e4 + 1) * 512], in_=pb),
                     wait=(nowv("t"),) if e4 == 0 else ())
            P.do("vector", lambda v: v.drain(), inc=(("v", 1),))
            P.do("sync", lambda e, w=w, r=r: e.dma_start(
                out=rs_in[w * WIN + r * 128:w * WIN + (r + 1) * 128, :], in_=optile),
                wait=(nowv("v"),), inc=(("dst", 16),))
        P.bar(f"w{w}")

    def phase_exchange2():
        P.do("gpsimd", lambda g: g.collective_compute(
            "ReduceScatter", ALU.add, replica_groups=[[0, 1], [2, 3], [4, 5], [6, 7]],
            ins=[rs_in[:, :]], outs=[rs_out[:, :]]), inc=(("cc", 1),))
        P.do("gpsimd", lambda g: None, wait=(nowv("cc"),))
        P.bar("e2")
        for i in range(16):
            P.do("sync", lambda e, i=i: e.dma_start(out=sq_dump, in_=x1bf[i * 128:(i + 1) * 128, :]),
                 wait=(nowv("v"),), inc=(("dls", 16),))
            P.do("sync", lambda e, i=i: e.dma_start(out=x2tile, in_=rs_out[i * 128:(i + 1) * 128, :]),
                 inc=(("dls", 16),))
            P.do("vector", lambda v: v.tensor_tensor(out=sq_dump, in0=sq_dump, in1=x2tile, op=ALU.add),
                 wait=(nowv("dls"),), inc=(("v", 1),))
            P.do("sync", lambda e, i=i: e.dma_start(out=x2bf[i * 128:(i + 1) * 128, :], in_=sq_dump),
                 wait=(nowv("v"),), inc=(("dst", 16),))
        P.bar("e2b")

    KPH = int(os.environ.get("K_PHASES", "99"))
    P.use("dls", "dls0")
    phase_weights()
    if KPH >= 2:
        P.use("dls", "dls1")
        phase_ffn(0)
    if KPH >= 3:
        phase_exchange1()
        attn_setup()
    if KPH >= 4:
        P.use("dls", "dls2")
        for w in range(min(NWIN, K_WINS)):
            if w == 4:
                P.use("dls", "dls3")
            attn_window(w)
    if KPH < 5:
        for i in range(16):
            P.do("sync", lambda e, i=i: e.dma_start(out=y_out[i * 128:(i + 1) * 128, :], in_=xt32),
                 wait=(P.now("v"),), inc=(("dst", 16),))
    if KPH >= 5 and DEBUG_DUMPS:
        for nm, src in (("rsin", rs_in), ("kw", kwv), ("qw", qwv), ("vw", vwv), ("nla", nlaw),
                        ("t1e", t1ew), ("gw", gww), ("og", ogw), ("Sq", Sq), ("xlnw", xlnw),
                        ("ahat", ahat), ("rash", rash), ("cs", cs_), ("gms", gm_s),
                        ("gqks", gqk_s), ("ms", ms_), ("p1s", p1s_), ("vt", vt_), ("rt", rt_),
                        ("uu", uu), ("oraw", oraw), ("ogs", ogs), ("ksscan", ks_scan),
                        ("acbc", acbc), ("betacol", betacol[:, 0:1].to_broadcast([128, 8]) if False else betacol[:, 0:1]), ("nlas", nlas),
                        ("lnwbc", lnwbc), ("lnbbc", lnbbc), ("olnt", olnt), ("gss", gs_),
                        ("lnrow", lnrow), ("onesc", consts["ones_1x128_f32"])):
            if src.dtype == F32:
                P.do("gpsimd", lambda g, nm=nm, src=src: g.dma_start(out=dbg[nm][:, :], in_=src),
                     inc=(("dlg", 16),))
            else:
                P.do("sync", lambda e, nm=nm, src=src: e.dma_start(out=dbg[nm][:, :], in_=src),
                     inc=(("dst", 16),))
        P.bar("dbgw")
    if KPH >= 5:
        phase_exchange2()
        P.use("dls", "dls4")
        phase_ffn(1)
    if KPH >= 5 and DEBUG_DUMPS:
        P.do("sync", lambda e: e.dma_start(out=dbg["x1"][:, :], in_=x1bf[:, :]), inc=(("dst", 16),))
        P.do("sync", lambda e: e.dma_start(out=dbg["ag"][:, :], in_=ag_out[:, :]), inc=(("dst", 16),))
        P.do("sync", lambda e: e.dma_start(out=dbg["rs"][:, :], in_=rs_out[:, :]), inc=(("dst", 16),))
        P.do("sync", lambda e: e.dma_start(out=dbg["x2"][:, :], in_=x2bf[:, :]), inc=(("dst", 16),))
    P.do("sync", lambda e: None, wait=(P.now("dst"),))

    with nc.Block() as block:
        @block.sync
        def _(e):
            P.replay("sync", e)

        @block.gpsimd
        def _(e):
            P.replay("gpsimd", e)

        @block.vector
        def _(e):
            P.replay("vector", e)

        @block.scalar
        def _(e):
            P.replay("scalar", e)

        @block.tensor
        def _(e):
            P.replay("tensor", e)

    stack.close()
    return nc


_NC_CACHE = None


def kernel(**inputs):
    global _NC_CACHE
    x = np.ascontiguousarray(np.asarray(inputs["L0_attn_x"], dtype=np.float32))
    B, TT, DD = x.shape
    down_idx = np.asarray(inputs["L0_ffn_down_reorder_index"])
    inv_down = np.argsort(down_idx)
    wd_eff = np.ascontiguousarray(np.asarray(inputs["L0_ffn_w_down"], np.float32)[:, inv_down])
    wg = np.ascontiguousarray(np.asarray(inputs["L0_ffn_w_gate"], np.float32))
    wu = np.ascontiguousarray(np.asarray(inputs["L0_ffn_w_up"], np.float32))
    A = lambda k: np.ascontiguousarray(np.asarray(inputs[f"L0_attn_{k}"], np.float32))
    Wq, Wk, Wv, Wa, Wb, Wg, Wo = A("Wq"), A("Wk"), A("Wv"), A("Wa"), A("Wb"), A("Wg"), A("Wo")
    ba, bb = A("ba"), A("bb")
    cw = {k: A(f"{k}conv_w") for k in "qkv"}
    cb = {k: A(f"{k}conv_b") for k in "qkv"}
    lnw, lnb = A("ln_w"), A("ln_b")

    in_maps = []
    for c in range(8):
        p, m = c // 2, c % 2
        hs = slice(m * HH * DH, (m + 1) * HH * DH)
        in_maps.append({
            "x0": np.ascontiguousarray(x[p, m * NT:(m + 1) * NT, :]),
            "ffn_wg": wg, "ffn_wu": wu, "ffn_wd": wd_eff,
            "aw_q": np.ascontiguousarray(Wq[hs]), "aw_k": np.ascontiguousarray(Wk[hs]),
            "aw_v": np.ascontiguousarray(Wv[hs]), "aw_a": np.ascontiguousarray(Wa[hs]),
            "aw_g": np.ascontiguousarray(Wg[hs]),
            "aw_b": np.ascontiguousarray(Wb[m * HH:(m + 1) * HH]),
            "aw_o": np.ascontiguousarray(Wo[:, hs]),
            "a_ba": np.ascontiguousarray(ba[hs]),
            "a_bb": np.ascontiguousarray(bb[m * HH:(m + 1) * HH]),
            "cw_q": np.ascontiguousarray(cw["q"][hs]), "cw_k": np.ascontiguousarray(cw["k"][hs]),
            "cw_v": np.ascontiguousarray(cw["v"][hs]),
            "cb_q": np.ascontiguousarray(cb["q"][hs]), "cb_k": np.ascontiguousarray(cb["k"][hs]),
            "cb_v": np.ascontiguousarray(cb["v"][hs]),
            "ln_w": lnw, "ln_b": lnb,
        })
    if _NC_CACHE is None:
        _NC_CACHE = build_nc()
    res = run_bass_kernel_spmd(_NC_CACHE, in_maps, core_ids=list(range(8)))
    out = np.empty((B, TT, DD), np.float32)
    for c in range(8):
        p, m = c // 2, c % 2
        out[p, m * NT:(m + 1) * NT, :] = res.results[c]["y"]
    return out
